# version 28
# baseline (speedup 1.0000x reference)
import contextlib
import numpy as np
import ml_dtypes
import concourse.bass as bass
import concourse.mybir as mybir
from concourse.bass_utils import run_bass_kernel_spmd

F32 = mybir.dt.float32
BF16 = mybir.dt.bfloat16
AF = mybir.ActivationFunctionType
ALU = mybir.AluOpType

T, D = 4096, 1024
NSB, SBT, NTT = 8, 512, 4
DECAY = 0.99
EPS = 1e-5
APP_SPLIT = False
TR_COPY_SPLIT = False

SEMS = ["pe", "dv", "ac", "gp", "lx", "gx", "st", "ld0", "ld1", "ld2", "ld3"]
ENG_OF_SEM = {"pe": "tensor", "dv": "vector", "ac": "scalar", "gp": "gpsimd",
              "lx": "sync", "gx": "sync", "st": "sync", "ld0": "sync",
              "ld1": "sync", "ld2": "sync", "ld3": "sync"}
ENG_SEM = {"tensor": "pe", "vector": "dv", "scalar": "ac", "gpsimd": "gp"}
COMPUTE_SEMS = ("pe", "dv", "ac", "gp")


class Res:
    __slots__ = ("name", "w", "r")

    def __init__(self, name):
        self.name, self.w, self.r = name, None, {}


class Plan:
    def __init__(self):
        self.items = {e: [] for e in ("tensor", "vector", "scalar", "sync", "gpsimd")}
        self.cnt = {s: 0 for s in SEMS}
        self.floor = {e: {} for e in self.items}
        self.ld_rr = 0
        self.marks = {e: [] for e in self.items}
        self.ncall = {e: 0 for e in self.items}

    def mark(self, label):
        for e in self.marks:
            self.marks[e].append((len(self.items[e]), label))

    def group(self, eng, fns, reads=(), writes=(), sem=None, dma=False, record=()):
        if sem is None:
            sem = ENG_SEM[eng]
        amt = 16 if dma else 1
        waits = {}

        def need(s, v):
            if s in COMPUTE_SEMS and ENG_OF_SEM[s] == eng:
                return
            if v > waits.get(s, 0):
                waits[s] = v

        ncalls = self.ncall[eng]
        for r in reads:
            if r.w is not None:
                need(r.w[0], r.w[1])
                if r.w[2] == eng and eng != "sync" and ncalls - r.w[3] < 2:
                    raise AssertionError(
                        "same-engine RAW too close: %s dist=%d eng=%s"
                        % (r.name, ncalls - r.w[3], eng))
        for r in writes:
            if r.w is not None:
                need(r.w[0], r.w[1])
            for s, v in r.r.items():
                need(s, v)
        if dma:
            if self.cnt[sem] > 0:
                need(sem, self.cnt[sem])
        fl = self.floor[eng]
        for s in sorted(waits):
            v = waits[s]
            if fl.get(s, 0) >= v:
                continue
            self.items[eng].append(("wait", s, v))
            fl[s] = v
        self.cnt[sem] += amt
        val = self.cnt[sem]
        for i, fn in enumerate(fns):
            self.items[eng].append(
                ("call", fn, (sem, amt) if i == len(fns) - 1 else None))
        self.ncall[eng] += len(fns)
        for r in writes:
            r.w = (sem, val, eng, self.ncall[eng])
            r.r = {}
        for r in reads:
            if val > r.r.get(sem, 0):
                r.r[sem] = val
        for r in record:
            if val > r.r.get(sem, 0):
                r.r[sem] = val
        return (sem, val)

    def dma(self, out_fn, in_fn, reads=(), writes=(), sem=None, eng="sync"):
        if sem is None:
            sem = "ld%d" % (self.ld_rr % 4)
            self.ld_rr += 1
        return self.group(eng, [lambda e: e.dma_start(out=out_fn(), in_=in_fn())],
                          reads=reads, writes=writes, sem=sem, dma=True)


LAST_PLAN = None


def build_nc(n_sb=NSB):
    global LAST_PLAN
    nc = bass.Bass("TRN2", target_bir_lowering=False, detect_race_conditions=False)
    dt = {}

    def din(name, shape, dty=F32):
        dt[name] = nc.dram_tensor(name, shape, dty, kind="ExternalInput").ap()

    din("xin", [128, T * D // 128])
    for w in ("wq", "wk", "wv", "wo", "wt"):
        din(w, [128, 8 * D])
    din("disha2", [128, D])
    din("bq", [128, 8])
    din("bk", [128, 8])
    for nm in ("dqh", "dql", "dkh", "dkl"):
        din(nm, [128, 32])
    din("tril", [128, 128], BF16)
    din("ident", [128, 128], BF16)
    out_d = nc.dram_tensor("out", [128, T * D // 128], F32, kind="ExternalOutput").ap()

    p = Plan()
    with contextlib.ExitStack() as ctx:
        S = {s: ctx.enter_context(nc.semaphore(s)) for s in SEMS}
        sb = {}

        def sbuf(name, shape, dty):
            sb[name] = ctx.enter_context(nc.sbuf_tensor("s_" + name, shape, dty))
            return sb[name]

        for w in ("wq", "wk", "wv", "wo", "wf"):
            sbuf(w, [128, 8 * D], BF16)
        S_sb = sbuf("S", [128, 8 * D], BF16)
        B = [sbuf("B%d" % i, [128, NTT * D], F32) for i in range(3)]
        ln_tm = sbuf("ln_tm", [128, NTT * D], BF16)
        lnf = sbuf("lnf", [128, 8, SBT], BF16)
        q_sb = sbuf("q", [128, 8 * SBT], BF16)
        kf_sb = sbuf("kf", [128, 8 * SBT], BF16)
        v_sb = sbuf("v", [128, NTT * D], BF16)
        sc_sb = sbuf("sc", [128, NTT * SBT], BF16)
        ya_sb = sbuf("ya", [128, 8 * SBT], BF16)
        dq_c = sbuf("dq_c", [128, 32], F32)
        dk_c = sbuf("dk_c", [128, 32], F32)
        ct0 = sbuf("ct0", [128, 32], F32)
        ct1 = sbuf("ct1", [128, 32], F32)
        bq_sb = sbuf("bq_sb", [128, 8], F32)
        bk_sb = sbuf("bk_sb", [128, 8], F32)
        tril_sb = sbuf("tril_sb", [128, 128], BF16)
        id_sb = sbuf("id_sb", [128, 128], BF16)
        dsh_sb = sbuf("dsh", [128, D], F32)
        ec = sbuf("ec", [128, 4], F32)
        stat = {tag: {nm: sbuf(tag + "_" + nm, [128, w], F32)
                      for nm, w in (("stats", 48), ("mv", 8), ("negm", 16),
                                    ("vpe", 4), ("rstd", 64), ("negms", 16))}
                for tag in ("a", "bc")}
        scr = sbuf("scr", [128, 4], F32)
        scrA = sbuf("scrA", [128, 4], F32)
        wide = sbuf("wide", [128, 512], F32)
        neghalf = sbuf("neghalf", [128, 4], F32)

        ps = [ctx.enter_context(nc.psum_tensor("p%d" % i, [128, 512], F32))
              for i in range(4)]
        pt = [ctx.enter_context(nc.psum_tensor("t%d" % i, [128, 1024], BF16))
              for i in range(4)]

        R = {n: Res(n) for n in
             list(sb) + ["p%d" % i for i in range(4)] + ["t%d" % i for i in range(4)]}
        rB = [R["B0"], R["B1"], R["B2"]]
        rps = [R["p%d" % i] for i in range(4)]
        rpt = [R["t%d" % i] for i in range(4)]
        r_ln_a = [Res("lna_%d" % i) for i in range(NTT)]
        r_ln_d = [Res("lnd_%d" % i) for i in range(NTT)]
        r_lnf = [Res("lnf_%d" % i) for i in range(NTT)]
        rstat = {tag: {nm: [Res(tag + nm + str(i)) for i in range(NTT)]
                       for nm in ("stats", "mv", "negm", "vpe", "rstd",
                                  "negms")}
                 for tag in ("a", "bc")}

        p.dma(lambda: B[0][:, :], lambda: dt["xin"][:, 0:NTT * D],
              writes=[rB[0]], sem="lx")
        for name, dest in (("ident", id_sb), ("tril", tril_sb), ("bq", bq_sb),
                           ("bk", bk_sb), ("disha2", dsh_sb)):
            p.dma(lambda d=dest: d[:, :], lambda n=name: dt[n][:, :],
                  writes=[R[{"ident": "id_sb", "tril": "tril_sb", "bq": "bq_sb",
                             "bk": "bk_sb", "disha2": "dsh"}[name]]])
        p.group("vector", [lambda e: e.memset(ec[:, :], EPS)], writes=[R["ec"]])
        p.group("vector", [lambda e: e.memset(neghalf[:, :], -0.5)],
                writes=[R["neghalf"]])
        p.group("vector", [lambda e: e.memset(wide[:, :], 0.0)],
                writes=[R["wide"]])

        slot_res = [Res("wslot%d" % i) for i in range(2)]

        def slot_ap(i):
            return B[2][:, i * 2048:(i + 1) * 2048]
        slot_rb = [rB[2], rB[2]]
        w_order = ("wt", "wq", "wk", "wv", "wo")
        w_scale = {"wq": 1.0, "wk": 1.0, "wv": 0.01, "wo": 1.0}
        w_eng = {"wq": "scalar", "wk": "scalar", "wv": "scalar", "wo": "vector"}

        LOADS = [(wi, q) for wi in range(5) for q in range(4)]
        dma_ptr = [0]
        cast_ptr = [0]

        def pump(k):
            for _ in range(k):
                if dma_ptr[0] >= len(LOADS):
                    return
                wi, q = LOADS[dma_ptr[0]]
                dma_ptr[0] += 1
                sl = (wi * 4 + q) % 2
                p.dma(lambda sl=sl: slot_ap(sl),
                      lambda w=w_order[wi], q=q: dt[w][:, q * 2048:(q + 1) * 2048],
                      writes=[slot_res[sl]])

        def cast_next(m):
            for _ in range(m):
                wi, q = LOADS[cast_ptr[0]]
                cast_ptr[0] += 1
                sl = (wi * 4 + q) % 2
                w = w_order[wi]
                if w == "wt":
                    p.group("vector",
                            [lambda e, sl=sl, q=q, cc=cc: e.scalar_tensor_tensor(
                                sb["wf"][:, q * 2048 + cc * D:q * 2048 + (cc + 1) * D],
                                slot_ap(sl)[:, cc * D:(cc + 1) * D], 1.0,
                                dsh_sb[:, :], ALU.mult, ALU.add)
                             for cc in range(2)],
                            reads=[slot_res[sl], R["dsh"]], record=[rB[2]],
                            writes=[R["wf"]])
                elif w_eng[w] == "scalar":
                    p.group("scalar",
                            [lambda e, sl=sl, w=w, q=q: e.activation(
                                sb[w][:, q * 2048:(q + 1) * 2048], slot_ap(sl),
                                AF.Copy, scale=w_scale[w])],
                            reads=[slot_res[sl]], record=[rB[2]], writes=[R[w]])
                else:
                    p.group("vector",
                            [lambda e, sl=sl, w=w, q=q: e.tensor_scalar(
                                sb[w][:, q * 2048:(q + 1) * 2048], slot_ap(sl),
                                w_scale[w], None, ALU.mult)],
                            reads=[slot_res[sl]], record=[rB[2]], writes=[R[w]])
                pump(1)

        XI = [0, 2, 1]
        HI = [1, 0, 2]

        class LN:

            def __init__(self, tag, xbuf, rx, dma_src=False, rx_tiles=None):
                self.st, self.rr = stat[tag], rstat[tag]
                self.x, self.rx, self.dma = xbuf, rx, dma_src
                self.rxt = rx_tiles

            def s1a(self, i):
                st, x = self.st, self.x
                p.group("vector",
                        [lambda e: e.bn_stats(st["stats"][:, 12 * i:12 * i + 6],
                                              x[:, i * D:i * D + 512])],
                        reads=([self.rxt[i]] if self.rxt else
                               ([self.rx] if self.dma else [])),
                        writes=[self.rr["stats"][i]])

            def s1b(self, i):
                st, x = self.st, self.x
                p.group("vector",
                        [lambda e: e.bn_stats(st["stats"][:, 12 * i + 6:12 * i + 12],
                                              x[:, i * D + 512:(i + 1) * D])],
                        reads=([self.rxt[i]] if self.rxt else
                               ([self.rx] if self.dma else [])),
                        writes=[self.rr["stats"][i]])

            def s2(self, i):
                st = self.st
                p.group("vector",
                        [lambda e: e.bn_aggr(st["mv"][:, 2 * i:2 * i + 2],
                                             st["stats"][:, 12 * i:12 * i + 12])],
                        reads=[self.rr["stats"][i]], writes=[self.rr["mv"][i]])

            def s3(self, i):
                st = self.st
                p.group("vector",
                        [lambda e: e.tensor_scalar(st["negm"][:, 4 * i:4 * i + 1],
                                                   st["mv"][:, 2 * i:2 * i + 1],
                                                   -1.0, None, ALU.mult),
                         lambda e: e.tensor_scalar(st["vpe"][:, i:i + 1],
                                                   st["mv"][:, 2 * i + 1:2 * i + 2],
                                                   EPS, None, ALU.add)],
                        reads=[self.rr["mv"][i]],
                        writes=[self.rr["negm"][i], self.rr["vpe"][i]])

            def pw(self, i):
                st = self.st
                p.group("gpsimd",
                        [lambda e: e.tensor_tensor(st["rstd"][:, 16 * i:16 * i + 1],
                                                   st["vpe"][:, i:i + 1],
                                                   neghalf[:, 0:1], ALU.pow)],
                        reads=[self.rr["vpe"][i], R["neghalf"]],
                        writes=[self.rr["rstd"][i]])

            def ngs(self, i):
                st = self.st
                p.group("scalar",
                        [lambda e: e.activation(st["negms"][:, 4 * i:4 * i + 1],
                                                st["negm"][:, 4 * i:4 * i + 1], AF.Identity,
                                                scale=st["rstd"][:, 16 * i:16 * i + 1])],
                        reads=[self.rr["negm"][i], self.rr["rstd"][i]],
                        writes=[self.rr["negms"][i]])

            def app_a(self, i):
                st, x = self.st, self.x
                p.group("scalar",
                        [lambda e: e.activation(
                            ln_tm[:, i * D:i * D + 512],
                            x[:, i * D:i * D + 512],
                            AF.Identity, bias=st["negms"][:, 4 * i:4 * i + 1],
                            scale=st["rstd"][:, 16 * i:16 * i + 1])],
                        reads=[self.rr["negms"][i], self.rr["rstd"][i]],
                        record=[self.rx], writes=[r_ln_a[i]])

            def app_d(self, i):
                st, x = self.st, self.x
                if APP_SPLIT:
                    p.group("vector",
                            [lambda e: e.tensor_scalar(
                                ln_tm[:, i * D + 512:(i + 1) * D],
                                x[:, i * D + 512:(i + 1) * D],
                                st["negm"][:, 4 * i:4 * i + 1], st["rstd"][:, 4 * i:4 * i + 1],
                                ALU.add, ALU.mult)],
                            reads=[self.rr["negm"][i], self.rr["rstd"][i]],
                            record=[self.rx], writes=[r_ln_d[i]])
                else:
                    p.group("scalar",
                            [lambda e: e.activation(
                                ln_tm[:, i * D + 512:(i + 1) * D],
                                x[:, i * D + 512:(i + 1) * D],
                                AF.Identity, bias=st["negms"][:, 4 * i:4 * i + 1],
                                scale=st["rstd"][:, 16 * i:16 * i + 1])],
                            reads=[self.rr["negms"][i], self.rr["rstd"][i]],
                            record=[self.rx], writes=[r_ln_d[i]])

            def pad(self):
                p.group("vector", [lambda e: e.memset(scr[:, 0:1], 0.0)])

            def acpad(self):
                p.group("scalar",
                        [lambda e: e.activation(scrA[:, 0:1], ec[:, 0:1],
                                                AF.Copy)],
                        reads=[R["ec"]])

            def pad_w(self):
                p.group("vector",
                        [lambda e: e.tensor_scalar(wide[:, 256:512],
                                                   wide[:, 0:256], 1.0, None,
                                                   ALU.mult)],
                        reads=[R["wide"]])

            def acpad_w(self):
                p.group("scalar",
                        [lambda e: e.activation(wide[:, 256:512],
                                                wide[:, 0:256], AF.Copy)],
                        reads=[R["wide"]])

            def feed2(self, i, s2):
                if (i, s2) == (0, 1):
                    self.pad(); self.pad(); self.s1a(0); self.s1b(0)
                elif (i, s2) == (1, 0):
                    self.pad(); self.s2(0)
                elif (i, s2) == (1, 1):
                    self.pad(); self.s3(0); self.s1a(1); self.s1b(1)
                    self.pw(0); self.ngs(0)
                elif (i, s2) == (2, 0):
                    self.pad(); self.s2(1)
                elif (i, s2) == (2, 1):
                    self.pad(); self.s3(1); self.s1a(2); self.s1b(2)
                    self.pw(1); self.ngs(1); self.acpad()
                    self.app_a(0); self.app_d(0)
                elif (i, s2) == (3, 0):
                    self.pad(); self.s2(2)
                elif (i, s2) == (3, 1):
                    self.pad(); self.s3(2); self.s1a(3); self.s1b(3)
                    self.pw(2); self.ngs(2); self.app_a(1); self.app_d(1)

            def flush(self):
                self.pad_w(); self.pad_w(); self.s2(3)
                self.app_a(2); self.app_d(2)
                self.pad_w(); self.pad_w(); self.s3(3); self.pw(3)
                self.ngs(3); self.acpad_w(); self.acpad_w()
                self.app_a(3); self.app_d(3)

            def standalone(self):
                self.s1a(0); self.s1b(0); self.s1a(1); self.s1b(1)
                self.s2(0); self.s1a(2); self.s1b(2); self.s2(1)
                self.s3(0); self.pw(0); self.s1a(3); self.s1b(3)
                self.s2(2); self.s3(1); self.pw(1); self.ngs(0)
                self.s2(3); self.s3(2); self.pw(2); self.ngs(1)
                self.acpad(); self.app_a(0); self.app_d(0)
                self.s3(3); self.pw(3); self.ngs(2)
                self.app_a(1); self.app_d(1); self.ngs(3)
                self.app_a(2); self.app_d(2); self.app_a(3); self.app_d(3)

        def tr_tile(i):
            if True:
                tp, ptb = rpt[i % 4], pt[i % 4]
                p.group("tensor",
                        [lambda e, c=c, i=i, ptb=ptb: e.transpose(
                            ptb[:, c * 128:(c + 1) * 128],
                            ln_tm[:, i * D + c * 128:i * D + (c + 1) * 128],
                            id_sb[:, :]) for c in range(8)],
                        reads=[r_ln_a[i], r_ln_d[i], R["id_sb"]], writes=[tp])
                p.group("vector",
                        [lambda e, i=i, ptb=ptb: e.tensor_copy(
                            lnf[:, :, i * 128:(i + 1) * 128], ptb[:, :])],
                        reads=[tp], writes=[r_lnf[i]])

        def tr_phase():
            for i in range(NTT):
                tr_tile(i)

        def ffn(dst, rdst, resid, rresid, ln, half_res=None, tile_cb=None):
            tr_tile(0)
            tr_tile(1)
            for i in range(NTT):
                for s2 in range(2):
                    j = (i * 2 + s2) % 4
                    pp, ppt = rps[j], ps[j]
                    p.group("tensor",
                            [lambda e, c=c, i=i, s2=s2, ppt=ppt: e.matmul(
                                ppt[:, :],
                                lnf[:, c, i * 128:(i + 1) * 128],
                                sb["wf"][:, c * D + s2 * 512:c * D + (s2 + 1) * 512],
                                start=(c == 0), stop=(c == 7)) for c in range(8)],
                            reads=[R["wf"], r_lnf[i]], writes=[pp])
                    sl = (i * D + s2 * 512, i * D + (s2 + 1) * 512)
                    p.group("vector",
                            [lambda e, sl=sl, ppt=ppt: e.scalar_tensor_tensor(
                                dst[:, sl[0]:sl[1]], ppt[:, :], 0.5,
                                resid[:, sl[0]:sl[1]], ALU.mult, ALU.add)],
                            reads=[pp, rresid],
                            writes=([rdst] + ([half_res[i // 2]]
                                              if half_res else [])))
                    if ln is not None:
                        ln.feed2(i, s2)
                if i + 2 < NTT:
                    tr_tile(i + 2)
                if tile_cb is not None:
                    tile_cb(i)
            if ln is not None:
                ln.flush()

        LN("a", B[0], rB[0], dma_src=True).standalone()
        for hi, lo, dst in (("dqh", "dql", "dq_c"), ("dkh", "dkl", "dk_c")):
            p.dma(lambda: ct0[:, :], lambda n=hi: dt[n][:, :], writes=[R["ct0"]])
            p.dma(lambda: ct1[:, :], lambda n=lo: dt[n][:, :], writes=[R["ct1"]])
            p.group("vector",
                    [lambda e, d=dst: e.tensor_tensor(sb[d][:, :], ct0[:, :],
                                                      ct1[:, :], ALU.add)],
                    reads=[R["ct0"], R["ct1"]], writes=[R[dst]])
        pump(2)
        cast_next(4)

        for n in range(n_sb):
            p.mark("S%d" % n)
            X, H = B[XI[n % 3]], B[HI[n % 3]]
            rX, rH = rB[XI[n % 3]], rB[HI[n % 3]]
            if 1 <= n and n + 1 < n_sb:
                PB, rPB = B[XI[(n + 1) % 3]], rB[XI[(n + 1) % 3]]
                p.dma(lambda pb=PB: pb[:, :],
                      lambda m=n + 1: dt["xin"][:, m * NTT * D:(m + 1) * NTT * D],
                      writes=[rPB], sem="lx")

            p.mark("trA")
            if n == 0:
                cast_next(4)
            p.mark("ffnA")
            ffn(H, rH, X, rX, LN("bc", H, rH))
            if n == 0:
                cast_next(4)

            p.mark("trB")
            tr_phase()
            if n == 0:
                cast_next(4)
            p.mark("qk")
            for dst, rdst, wname, bias in ((q_sb, R["q"], "wq", bq_sb),
                                           (kf_sb, R["kf"], "wk", bk_sb)):
                for c2 in range(8):
                    pp, ppt = rps[c2 % 4], ps[c2 % 4]
                    p.group("tensor",
                            [lambda e, c=c, c2=c2, wname=wname, ppt=ppt: e.matmul(
                                ppt[:, :],
                                sb[wname][:, c * D + c2 * 128:c * D + (c2 + 1) * 128],
                                lnf[:, c, :],
                                start=(c == 0), stop=(c == 7)) for c in range(8)],
                            reads=[R[wname]] + r_lnf, writes=[pp])
                    p.group("scalar",
                            [lambda e, c2=c2, dst=dst, bias=bias, ppt=ppt: e.activation(
                                dst[:, c2 * 512:(c2 + 1) * 512], ppt[:, :], AF.Sigmoid,
                                bias=bias[:, c2:c2 + 1])],
                            reads=[pp, R["bq_sb"], R["bk_sb"]], writes=[rdst])
            p.mark("v")
            for i in range(NTT):
                for s2 in range(2):
                    j = (i * 2 + s2) % 4
                    pp, ppt = rps[j], ps[j]
                    p.group("tensor",
                            [lambda e, c=c, i=i, s2=s2, ppt=ppt: e.matmul(
                                ppt[:, :],
                                lnf[:, c, i * 128:(i + 1) * 128],
                                sb["wv"][:, c * D + s2 * 512:c * D + (s2 + 1) * 512],
                                start=(c == 0), stop=(c == 7)) for c in range(8)],
                            reads=[R["wv"], r_lnf[i]], writes=[pp])
                    p.group("scalar",
                            [lambda e, i=i, s2=s2, ppt=ppt: e.activation(
                                v_sb[:, i * D + s2 * 512:i * D + (s2 + 1) * 512],
                                ppt[:, :], AF.Copy)],
                            reads=[pp], writes=[R["v"]])
            if n == 0:
                cast_next(4)
                if n_sb > 1:
                    p.dma(lambda: B[2][:, :],
                          lambda: dt["xin"][:, NTT * D:2 * NTT * D],
                          reads=[slot_res[0], slot_res[1]], writes=[rB[2]],
                          sem="lx")
            p.mark("kt")
            for g in range(NTT):
                tp, ptb = rpt[g % 4], pt[g % 4]
                p.group("tensor",
                        [lambda e, c2=c2, g=g, ptb=ptb: e.transpose(
                            ptb[:, c2 * 128:(c2 + 1) * 128],
                            kf_sb[:, c2 * 512 + g * 128:c2 * 512 + (g + 1) * 128],
                            id_sb[:, :]) for c2 in range(8)],
                        reads=[R["kf"], R["id_sb"]], writes=[tp])
                col = n * NTT + g
                p.group("scalar",
                        [lambda e, g=g, h2=h2, ptb=ptb, col=col: e.activation(
                            lnf[:, 2 * g + h2, :], ptb[:, h2 * 512:(h2 + 1) * 512],
                            AF.Identity, scale=dk_c[:, col:col + 1])
                         for h2 in range(2)],
                        reads=[tp, R["dk_c"]], writes=[r_lnf[g]])
            p.mark("sc")
            for j in range(NTT):
                pp, ppt = rps[j % 4], ps[j % 4]
                p.group("tensor",
                        [lambda e, c2=c2, j=j, ppt=ppt: e.matmul(
                            ppt[:, j * 128:512],
                            kf_sb[:, c2 * 512 + j * 128:c2 * 512 + (j + 1) * 128],
                            q_sb[:, c2 * 512 + j * 128:(c2 + 1) * 512],
                            start=(c2 == 0), stop=(c2 == 7)) for c2 in range(8)],
                        reads=[R["kf"], R["q"]], writes=[pp])
                col = n * NTT + j
                fns = []
                fns.append(lambda e, j=j, col=col, ppt=ppt: e.scalar_tensor_tensor(
                    sc_sb[:, j * 512 + j * 128:j * 512 + (j + 1) * 128],
                    ppt[:, j * 128:(j + 1) * 128], dk_c[:, col:col + 1],
                    tril_sb[:, :], ALU.mult, ALU.mult))
                if j < 3:
                    fns.append(lambda e, j=j, col=col, ppt=ppt: e.tensor_scalar(
                        sc_sb[:, j * 512 + (j + 1) * 128:(j + 1) * 512],
                        ppt[:, (j + 1) * 128:512], dk_c[:, col:col + 1], None,
                        ALU.mult))
                p.group("vector", fns, reads=[pp, R["dk_c"], R["tril_sb"]],
                        writes=[R["sc"]])
            p.mark("y")
            for m in range(8):
                pp, ppt = rps[m % 4], ps[m % 4]
                fns = [lambda e, jj=jj, m=m, ppt=ppt, n=n: e.matmul(
                    ppt[:, jj * 128:512],
                    v_sb[:, jj * D + m * 128:jj * D + (m + 1) * 128],
                    sc_sb[:, jj * 512 + jj * 128:(jj + 1) * 512],
                    start=(jj == 0), stop=(jj == 3 and n == 0)) for jj in range(NTT)]
                if n > 0:
                    fns += [lambda e, c2=c2, m=m, ppt=ppt: e.matmul(
                        ppt[:, :], S_sb[:, c2 * D + m * 128:c2 * D + (m + 1) * 128],
                        q_sb[:, c2 * 512:(c2 + 1) * 512],
                        start=False, stop=(c2 == 7)) for c2 in range(8)]
                p.group("tensor", fns, reads=[R["v"], R["sc"], R["S"], R["q"]],
                        writes=[pp])
                p.group("vector",
                        [lambda e, m=m, ppt=ppt: e.tensor_copy(
                            ya_sb[:, m * 512:(m + 1) * 512], ppt[:, :])],
                        reads=[pp], writes=[R["ya"]])
            p.mark("kv")
            if n + 1 < n_sb:
                for mk in range(8):
                    for s2 in range(2):
                        j2 = (mk * 2 + s2) % 4
                        pp, ppt = rps[j2], ps[j2]
                        p.group("tensor",
                                [lambda e, jj=jj, mk=mk, s2=s2, ppt=ppt: e.matmul(
                                    ppt[:, :],
                                    lnf[:, 2 * jj + mk // 4,
                                        (mk % 4) * 128:(mk % 4 + 1) * 128],
                                    v_sb[:, jj * D + s2 * 512:jj * D + (s2 + 1) * 512],
                                    start=(jj == 0), stop=(jj == 3))
                                 for jj in range(NTT)],
                                reads=r_lnf + [R["v"]], writes=[pp])
                        sl = (mk * D + s2 * 512, mk * D + (s2 + 1) * 512)
                        if n == 0:
                            p.group("vector",
                                    [lambda e, sl=sl, ppt=ppt: e.tensor_copy(
                                        S_sb[:, sl[0]:sl[1]], ppt[:, :])],
                                    reads=[pp], writes=[R["S"]])
                        else:
                            p.group("vector",
                                    [lambda e, sl=sl, ppt=ppt: e.tensor_tensor(
                                        S_sb[:, sl[0]:sl[1]], ppt[:, :],
                                        S_sb[:, sl[0]:sl[1]], ALU.add)],
                                    reads=[pp], writes=[R["S"]])
            p.mark("wo")
            ln_c = LN("bc", X, rX)
            for i in range(NTT):
                for s2 in range(2):
                    j2 = (i * 2 + s2) % 4
                    pp, ppt = rps[j2], ps[j2]
                    p.group("tensor",
                            [lambda e, c2=c2, i=i, s2=s2, ppt=ppt: e.matmul(
                                ppt[:, :],
                                ya_sb[:, c2 * 512 + i * 128:c2 * 512 + (i + 1) * 128],
                                sb["wo"][:, c2 * D + s2 * 512:c2 * D + (s2 + 1) * 512],
                                start=(c2 == 0), stop=(c2 == 7)) for c2 in range(8)],
                            reads=[R["ya"], R["wo"]], writes=[pp])
                    col = n * NTT + i
                    sl = (i * D + s2 * 512, i * D + (s2 + 1) * 512)
                    p.group("vector",
                            [lambda e, sl=sl, col=col, ppt=ppt, X=X, H=H:
                             e.scalar_tensor_tensor(
                                X[:, sl[0]:sl[1]], ppt[:, :], dq_c[:, col:col + 1],
                                H[:, sl[0]:sl[1]], ALU.mult, ALU.add)],
                            reads=[pp, rH, R["dq_c"]], writes=[rX])
                    ln_c.feed2(i, s2)
            ln_c.flush()

            p.mark("trC")
            p.mark("ffnC")
            if n + 1 < n_sb:
                PB, rPB = B[XI[(n + 1) % 3]], rB[XI[(n + 1) % 3]]
                ln_a = LN("a", PB, rPB, dma_src=True)
            else:
                ln_a = None
            rH2 = [Res("H%d_h%d" % (n, h)) for h in range(2)]

            def store_cb(i, n=n, H=H, rH2=rH2):
                if i in (1, 3):
                    h = i // 2
                    p.dma(lambda n=n, h=h: out_d[:, n * NTT * D + h * 2 * D:
                                                 n * NTT * D + (h + 1) * 2 * D],
                          lambda H=H, h=h: H[:, h * 2 * D:(h + 1) * 2 * D],
                          reads=[rH2[h]], sem="st")
            ffn(H, rH, X, rX, ln_a, half_res=rH2, tile_cb=store_cb)

        p.items["sync"].append(("wait", "st", p.cnt["st"]))
        LAST_PLAN = p

        with nc.Block() as block:
            for eng in ("sync", "tensor", "vector", "scalar", "gpsimd"):
                items = p.items[eng]
                if not items:
                    continue

                def body(e, items=items):
                    for it in items:
                        if it[0] == "wait":
                            e.wait_ge(S[it[1]], it[2])
                        else:
                            inst = it[1](e)
                            if it[2] is not None:
                                inst.then_inc(S[it[2][0]], it[2][1])
                getattr(block, eng)(body)
    return nc


def _tile128(w):
    n = w.shape[1]
    return np.ascontiguousarray(
        w.reshape(8, 128, n).transpose(1, 0, 2).reshape(128, 8 * n))


def _hilo(vals64):
    c = np.ascontiguousarray(vals64.reshape(32, 128).T)
    hi = c.astype(np.float32)
    lo = (c - hi.astype(np.float64)).astype(np.float32)
    return hi, lo


def kernel(L0_attn_x, L0_attn_Wv, L0_attn_Wk, L0_attn_bk, L0_attn_Wq,
           L0_attn_bq, L0_attn_Wo, L0_ffn_weight, L0_ffn_disha):
    bf16 = ml_dtypes.bfloat16
    x = np.asarray(L0_attn_x, dtype=np.float32)
    wq = _tile128(np.asarray(L0_attn_Wq, dtype=np.float32))
    wk = _tile128(np.asarray(L0_attn_Wk, dtype=np.float32))
    wv = _tile128(np.asarray(L0_attn_Wv, dtype=np.float32))
    wo = _tile128(np.asarray(L0_attn_Wo, dtype=np.float32))
    wt = _tile128(np.ascontiguousarray(np.asarray(L0_ffn_weight, dtype=np.float32).T))
    disha = np.asarray(L0_ffn_disha, dtype=np.float32)
    disha2 = np.ascontiguousarray(np.concatenate([disha, disha], axis=0))
    bq = np.ascontiguousarray(np.asarray(L0_attn_bq, np.float32).reshape(8, 128).T)
    bk = np.ascontiguousarray(np.asarray(L0_attn_bk, np.float32).reshape(8, 128).T)
    s64 = np.arange(T, dtype=np.float64)
    dqh, dql = _hilo(np.float64(DECAY) ** s64)
    dkh, dkl = _hilo(np.float64(DECAY) ** (-s64))
    tril = np.triu(np.ones((128, 128))).astype(bf16)
    ident = np.eye(128).astype(bf16)

    nc = build_nc()
    in_maps = []
    for c in range(8):
        xc = np.ascontiguousarray(
            x[c].reshape(32, 128, D).transpose(1, 0, 2).reshape(128, 32 * D))
        in_maps.append({"xin": xc, "wq": wq, "wk": wk, "wv": wv, "wo": wo,
                        "wt": wt, "disha2": disha2, "bq": bq, "bk": bk,
                        "dqh": dqh, "dql": dql, "dkh": dkh, "dkl": dkl,
                        "tril": tril, "ident": ident})
    res = run_bass_kernel_spmd(nc, in_maps, core_ids=list(range(8)))
    outs = []
    for r in res.results:
        o = np.asarray(r["out"]).reshape(128, 32, D).transpose(1, 0, 2)
        outs.append(o.reshape(T, D))
    return np.stack(outs).astype(np.float32)


if __name__ == "__main__":
    import reference
    ins = reference.setup_inputs()
    out = kernel(**{k: np.asarray(v) for k, v in ins.items()})
    ref = np.asarray(reference.reference(**ins))
    err = np.linalg.norm(out - ref) / np.linalg.norm(ref)
    print("rel_err", err)


# version 29
# speedup vs baseline: 1.0131x; 1.0131x over previous
import contextlib
import numpy as np
import ml_dtypes
import concourse.bass as bass
import concourse.mybir as mybir
from concourse.bass_utils import run_bass_kernel_spmd

F32 = mybir.dt.float32
BF16 = mybir.dt.bfloat16
AF = mybir.ActivationFunctionType
ALU = mybir.AluOpType

T, D = 4096, 1024
NSB, SBT, NTT = 8, 512, 4
DECAY = 0.99
EPS = 1e-5
APP_SPLIT = False
TR_COPY_SPLIT = False

SEMS = ["pe", "dv", "ac", "gp", "lx", "gx", "st", "ld0", "ld1", "ld2", "ld3"]
ENG_OF_SEM = {"pe": "tensor", "dv": "vector", "ac": "scalar", "gp": "gpsimd",
              "lx": "sync", "gx": "sync", "st": "sync", "ld0": "sync",
              "ld1": "sync", "ld2": "sync", "ld3": "sync"}
ENG_SEM = {"tensor": "pe", "vector": "dv", "scalar": "ac", "gpsimd": "gp"}
COMPUTE_SEMS = ("pe", "dv", "ac", "gp")


class Res:
    __slots__ = ("name", "w", "r")

    def __init__(self, name):
        self.name, self.w, self.r = name, None, {}


class Plan:
    def __init__(self):
        self.items = {e: [] for e in ("tensor", "vector", "scalar", "sync", "gpsimd")}
        self.cnt = {s: 0 for s in SEMS}
        self.floor = {e: {} for e in self.items}
        self.ld_rr = 0
        self.marks = {e: [] for e in self.items}
        self.ncall = {e: 0 for e in self.items}

    def mark(self, label):
        for e in self.marks:
            self.marks[e].append((len(self.items[e]), label))

    def group(self, eng, fns, reads=(), writes=(), sem=None, dma=False, record=()):
        if sem is None:
            sem = ENG_SEM[eng]
        amt = 16 if dma else 1
        waits = {}

        def need(s, v):
            if s in COMPUTE_SEMS and ENG_OF_SEM[s] == eng:
                return
            if v > waits.get(s, 0):
                waits[s] = v

        ncalls = self.ncall[eng]
        for r in reads:
            if r.w is not None:
                need(r.w[0], r.w[1])
                if r.w[2] == eng and eng != "sync" and ncalls - r.w[3] < 2:
                    raise AssertionError(
                        "same-engine RAW too close: %s dist=%d eng=%s"
                        % (r.name, ncalls - r.w[3], eng))
        for r in writes:
            if r.w is not None:
                need(r.w[0], r.w[1])
            for s, v in r.r.items():
                need(s, v)
        if dma:
            if self.cnt[sem] > 0:
                need(sem, self.cnt[sem])
        fl = self.floor[eng]
        for s in sorted(waits):
            v = waits[s]
            if fl.get(s, 0) >= v:
                continue
            self.items[eng].append(("wait", s, v))
            fl[s] = v
        self.cnt[sem] += amt
        val = self.cnt[sem]
        for i, fn in enumerate(fns):
            self.items[eng].append(
                ("call", fn, (sem, amt) if i == len(fns) - 1 else None))
        self.ncall[eng] += len(fns)
        for r in writes:
            r.w = (sem, val, eng, self.ncall[eng])
            r.r = {}
        for r in reads:
            if val > r.r.get(sem, 0):
                r.r[sem] = val
        for r in record:
            if val > r.r.get(sem, 0):
                r.r[sem] = val
        return (sem, val)

    def dma(self, out_fn, in_fn, reads=(), writes=(), sem=None, eng="sync"):
        if sem is None:
            sem = "ld%d" % (self.ld_rr % 4)
            self.ld_rr += 1
        return self.group(eng, [lambda e: e.dma_start(out=out_fn(), in_=in_fn())],
                          reads=reads, writes=writes, sem=sem, dma=True)


LAST_PLAN = None


def build_nc(n_sb=NSB):
    global LAST_PLAN
    nc = bass.Bass("TRN2", target_bir_lowering=False, detect_race_conditions=False)
    dt = {}

    def din(name, shape, dty=F32):
        dt[name] = nc.dram_tensor(name, shape, dty, kind="ExternalInput").ap()

    din("xin", [128, T * D // 128])
    for w in ("wq", "wk", "wv", "wo", "wt"):
        din(w, [128, 8 * D])
    din("disha2", [128, D])
    din("bq", [128, 8])
    din("bk", [128, 8])
    for nm in ("dqh", "dql", "dkh", "dkl"):
        din(nm, [128, 32])
    din("tril", [128, 128], BF16)
    din("ident", [128, 128], BF16)
    out_d = nc.dram_tensor("out", [128, T * D // 128], F32, kind="ExternalOutput").ap()

    p = Plan()
    with contextlib.ExitStack() as ctx:
        S = {s: ctx.enter_context(nc.semaphore(s)) for s in SEMS}
        sb = {}

        def sbuf(name, shape, dty):
            sb[name] = ctx.enter_context(nc.sbuf_tensor("s_" + name, shape, dty))
            return sb[name]

        for w in ("wq", "wk", "wv", "wo", "wf"):
            sbuf(w, [128, 8 * D], BF16)
        S_sb = sbuf("S", [128, 8 * D], BF16)
        B = [sbuf("B%d" % i, [128, NTT * D], F32) for i in range(3)]
        ln_tm = sbuf("ln_tm", [128, NTT * D], BF16)
        lnf = sbuf("lnf", [128, 8, SBT], BF16)
        q_sb = sbuf("q", [128, 8 * SBT], BF16)
        kf_sb = sbuf("kf", [128, 8 * SBT], BF16)
        v_sb = sbuf("v", [128, NTT * D], BF16)
        sc_sb = sbuf("sc", [128, NTT * SBT], BF16)
        ya_sb = sbuf("ya", [128, 8 * SBT], BF16)
        dq_c = sbuf("dq_c", [128, 32], F32)
        dk_c = sbuf("dk_c", [128, 32], F32)
        ct0 = sbuf("ct0", [128, 32], F32)
        ct1 = sbuf("ct1", [128, 32], F32)
        bq_sb = sbuf("bq_sb", [128, 8], F32)
        bk_sb = sbuf("bk_sb", [128, 8], F32)
        tril_sb = sbuf("tril_sb", [128, 128], BF16)
        id_sb = sbuf("id_sb", [128, 128], BF16)
        dsh_sb = sbuf("dsh", [128, D], F32)
        ec = sbuf("ec", [128, 4], F32)
        stat = {tag: {nm: sbuf(tag + "_" + nm, [128, w], F32)
                      for nm, w in (("stats", 48), ("mv", 8), ("negm", 16),
                                    ("vpe", 4), ("rstd", 64), ("negms", 16))}
                for tag in ("a", "bc")}
        scr = sbuf("scr", [128, 4], F32)
        scrA = sbuf("scrA", [128, 4], F32)
        wide = sbuf("wide", [128, 512], F32)
        neghalf = sbuf("neghalf", [128, 4], F32)

        ps = [ctx.enter_context(nc.psum_tensor("p%d" % i, [128, 512], F32))
              for i in range(4)]
        pt = [ctx.enter_context(nc.psum_tensor("t%d" % i, [128, 1024], BF16))
              for i in range(4)]

        R = {n: Res(n) for n in
             list(sb) + ["p%d" % i for i in range(4)] + ["t%d" % i for i in range(4)]}
        rB = [R["B0"], R["B1"], R["B2"]]
        rps = [R["p%d" % i] for i in range(4)]
        rpt = [R["t%d" % i] for i in range(4)]
        r_ln_a = [Res("lna_%d" % i) for i in range(NTT)]
        r_ln_d = [Res("lnd_%d" % i) for i in range(NTT)]
        r_lnf = [Res("lnf_%d" % i) for i in range(NTT)]
        rstat = {tag: {nm: [Res(tag + nm + str(i)) for i in range(NTT)]
                       for nm in ("stats", "mv", "negm", "vpe", "rstd",
                                  "negms")}
                 for tag in ("a", "bc")}

        p.dma(lambda: B[0][:, :], lambda: dt["xin"][:, 0:NTT * D],
              writes=[rB[0]], sem="lx")
        for name, dest in (("ident", id_sb), ("tril", tril_sb), ("bq", bq_sb),
                           ("bk", bk_sb), ("disha2", dsh_sb)):
            p.dma(lambda d=dest: d[:, :], lambda n=name: dt[n][:, :],
                  writes=[R[{"ident": "id_sb", "tril": "tril_sb", "bq": "bq_sb",
                             "bk": "bk_sb", "disha2": "dsh"}[name]]])
        p.group("vector", [lambda e: e.memset(ec[:, :], EPS)], writes=[R["ec"]])
        p.group("vector", [lambda e: e.memset(neghalf[:, :], -0.5)],
                writes=[R["neghalf"]])
        p.group("vector", [lambda e: e.memset(wide[:, :], 0.0)],
                writes=[R["wide"]])

        slot_res = [Res("wslot%d" % i) for i in range(2)]

        def slot_ap(i):
            return B[2][:, i * 2048:(i + 1) * 2048]
        slot_rb = [rB[2], rB[2]]
        w_order = ("wt", "wq", "wk", "wv", "wo")
        w_scale = {"wq": 1.0, "wk": 1.0, "wv": 0.01, "wo": 1.0}
        w_eng = {"wq": "scalar", "wk": "scalar", "wv": "scalar", "wo": "vector"}

        LOADS = [(wi, q) for wi in range(5) for q in range(4)]
        dma_ptr = [0]
        cast_ptr = [0]

        def pump(k):
            for _ in range(k):
                if dma_ptr[0] >= len(LOADS):
                    return
                wi, q = LOADS[dma_ptr[0]]
                dma_ptr[0] += 1
                sl = (wi * 4 + q) % 2
                p.dma(lambda sl=sl: slot_ap(sl),
                      lambda w=w_order[wi], q=q: dt[w][:, q * 2048:(q + 1) * 2048],
                      writes=[slot_res[sl]])

        def cast_next(m):
            for _ in range(m):
                wi, q = LOADS[cast_ptr[0]]
                cast_ptr[0] += 1
                sl = (wi * 4 + q) % 2
                w = w_order[wi]
                if w == "wt":
                    p.group("vector",
                            [lambda e, sl=sl, q=q, cc=cc: e.scalar_tensor_tensor(
                                sb["wf"][:, q * 2048 + cc * D:q * 2048 + (cc + 1) * D],
                                slot_ap(sl)[:, cc * D:(cc + 1) * D], 1.0,
                                dsh_sb[:, :], ALU.mult, ALU.add)
                             for cc in range(2)],
                            reads=[slot_res[sl], R["dsh"]], record=[rB[2]],
                            writes=[R["wf"]])
                elif w_eng[w] == "scalar":
                    p.group("scalar",
                            [lambda e, sl=sl, w=w, q=q: e.activation(
                                sb[w][:, q * 2048:(q + 1) * 2048], slot_ap(sl),
                                AF.Copy, scale=w_scale[w])],
                            reads=[slot_res[sl]], record=[rB[2]], writes=[R[w]])
                else:
                    p.group("vector",
                            [lambda e, sl=sl, w=w, q=q: e.tensor_scalar(
                                sb[w][:, q * 2048:(q + 1) * 2048], slot_ap(sl),
                                w_scale[w], None, ALU.mult)],
                            reads=[slot_res[sl]], record=[rB[2]], writes=[R[w]])
                pump(1)

        XI = [0, 2, 1]
        HI = [1, 0, 2]

        class LN:

            def __init__(self, tag, xbuf, rx, dma_src=False, rx_tiles=None):
                self.st, self.rr = stat[tag], rstat[tag]
                self.x, self.rx, self.dma = xbuf, rx, dma_src
                self.rxt = rx_tiles

            def s1a(self, i):
                st, x = self.st, self.x
                p.group("vector",
                        [lambda e: e.bn_stats(st["stats"][:, 12 * i:12 * i + 6],
                                              x[:, i * D:i * D + 512])],
                        reads=([self.rxt[i]] if self.rxt else
                               ([self.rx] if self.dma else [])),
                        writes=[self.rr["stats"][i]])

            def s1b(self, i):
                st, x = self.st, self.x
                p.group("vector",
                        [lambda e: e.bn_stats(st["stats"][:, 12 * i + 6:12 * i + 12],
                                              x[:, i * D + 512:(i + 1) * D])],
                        reads=([self.rxt[i]] if self.rxt else
                               ([self.rx] if self.dma else [])),
                        writes=[self.rr["stats"][i]])

            def s2(self, i):
                st = self.st
                p.group("vector",
                        [lambda e: e.bn_aggr(st["mv"][:, 2 * i:2 * i + 2],
                                             st["stats"][:, 12 * i:12 * i + 12])],
                        reads=[self.rr["stats"][i]], writes=[self.rr["mv"][i]])

            def s3(self, i):
                st = self.st
                p.group("vector",
                        [lambda e: e.tensor_scalar(st["negm"][:, 4 * i:4 * i + 1],
                                                   st["mv"][:, 2 * i:2 * i + 1],
                                                   -1.0, None, ALU.mult),
                         lambda e: e.tensor_scalar(st["vpe"][:, i:i + 1],
                                                   st["mv"][:, 2 * i + 1:2 * i + 2],
                                                   EPS, None, ALU.add)],
                        reads=[self.rr["mv"][i]],
                        writes=[self.rr["negm"][i], self.rr["vpe"][i]])

            def pw(self, i):
                st = self.st
                p.group("gpsimd",
                        [lambda e: e.tensor_tensor(st["rstd"][:, 16 * i:16 * i + 1],
                                                   st["vpe"][:, i:i + 1],
                                                   neghalf[:, 0:1], ALU.pow)],
                        reads=[self.rr["vpe"][i], R["neghalf"]],
                        writes=[self.rr["rstd"][i]])

            def ngs(self, i):
                st = self.st
                p.group("scalar",
                        [lambda e: e.activation(st["negms"][:, 4 * i:4 * i + 1],
                                                st["negm"][:, 4 * i:4 * i + 1], AF.Identity,
                                                scale=st["rstd"][:, 16 * i:16 * i + 1])],
                        reads=[self.rr["negm"][i], self.rr["rstd"][i]],
                        writes=[self.rr["negms"][i]])

            def app_a(self, i):
                st, x = self.st, self.x
                p.group("scalar",
                        [lambda e: e.activation(
                            ln_tm[:, i * D:i * D + 512],
                            x[:, i * D:i * D + 512],
                            AF.Identity, bias=st["negms"][:, 4 * i:4 * i + 1],
                            scale=st["rstd"][:, 16 * i:16 * i + 1])],
                        reads=[self.rr["negms"][i], self.rr["rstd"][i]],
                        record=[self.rx], writes=[r_ln_a[i]])

            def app_d(self, i):
                st, x = self.st, self.x
                if APP_SPLIT:
                    p.group("vector",
                            [lambda e: e.tensor_scalar(
                                ln_tm[:, i * D + 512:(i + 1) * D],
                                x[:, i * D + 512:(i + 1) * D],
                                st["negm"][:, 4 * i:4 * i + 1], st["rstd"][:, 4 * i:4 * i + 1],
                                ALU.add, ALU.mult)],
                            reads=[self.rr["negm"][i], self.rr["rstd"][i]],
                            record=[self.rx], writes=[r_ln_d[i]])
                else:
                    p.group("scalar",
                            [lambda e: e.activation(
                                ln_tm[:, i * D + 512:(i + 1) * D],
                                x[:, i * D + 512:(i + 1) * D],
                                AF.Identity, bias=st["negms"][:, 4 * i:4 * i + 1],
                                scale=st["rstd"][:, 16 * i:16 * i + 1])],
                            reads=[self.rr["negms"][i], self.rr["rstd"][i]],
                            record=[self.rx], writes=[r_ln_d[i]])

            def pad(self):
                p.group("vector", [lambda e: e.memset(scr[:, 0:1], 0.0)])

            def acpad(self):
                p.group("scalar",
                        [lambda e: e.activation(scrA[:, 0:1], ec[:, 0:1],
                                                AF.Copy)],
                        reads=[R["ec"]])

            def pad_w(self):
                p.group("vector",
                        [lambda e: e.tensor_scalar(wide[:, 256:512],
                                                   wide[:, 0:256], 1.0, None,
                                                   ALU.mult)],
                        reads=[R["wide"]])

            def acpad_w(self):
                p.group("scalar",
                        [lambda e: e.activation(wide[:, 256:512],
                                                wide[:, 0:256], AF.Copy)],
                        reads=[R["wide"]])

            def feed2(self, i, s2):
                if (i, s2) == (0, 1):
                    self.pad(); self.pad(); self.s1a(0); self.s1b(0)
                elif (i, s2) == (1, 0):
                    self.pad(); self.s2(0)
                elif (i, s2) == (1, 1):
                    self.pad(); self.s3(0); self.s1a(1); self.s1b(1)
                    self.pw(0); self.ngs(0)
                elif (i, s2) == (2, 0):
                    self.pad(); self.s2(1)
                elif (i, s2) == (2, 1):
                    self.pad(); self.s3(1); self.s1a(2); self.s1b(2)
                    self.pw(1); self.ngs(1); self.acpad()
                    self.app_a(0); self.app_d(0)
                elif (i, s2) == (3, 0):
                    self.pad(); self.s2(2)
                elif (i, s2) == (3, 1):
                    self.pad(); self.s3(2); self.s1a(3); self.s1b(3)
                    self.pw(2); self.ngs(2); self.app_a(1); self.app_d(1)

            def flush(self):
                self.pad_w(); self.pad_w(); self.s2(3)
                self.app_a(2); self.app_d(2)
                self.pad_w(); self.pad_w(); self.s3(3); self.pw(3)
                self.ngs(3); self.acpad_w(); self.acpad_w()
                self.app_a(3); self.app_d(3)

            def standalone(self):
                self.s1a(0); self.s1b(0); self.s1a(1); self.s1b(1)
                self.s2(0); self.s1a(2); self.s1b(2); self.s2(1)
                self.s3(0); self.pw(0); self.s1a(3); self.s1b(3)
                self.s2(2); self.s3(1); self.pw(1); self.ngs(0)
                self.s2(3); self.s3(2); self.pw(2); self.ngs(1)
                self.acpad(); self.app_a(0); self.app_d(0)
                self.s3(3); self.pw(3); self.ngs(2)
                self.app_a(1); self.app_d(1); self.ngs(3)
                self.app_a(2); self.app_d(2); self.app_a(3); self.app_d(3)

        def tr_tile(i):
            if True:
                tp, ptb = rpt[i % 4], pt[i % 4]
                p.group("tensor",
                        [lambda e, c=c, i=i, ptb=ptb: e.transpose(
                            ptb[:, c * 128:(c + 1) * 128],
                            ln_tm[:, i * D + c * 128:i * D + (c + 1) * 128],
                            id_sb[:, :]) for c in range(8)],
                        reads=[r_ln_a[i], r_ln_d[i], R["id_sb"]], writes=[tp])
                p.group("vector",
                        [lambda e, i=i, ptb=ptb: e.tensor_copy(
                            lnf[:, :, i * 128:(i + 1) * 128], ptb[:, :])],
                        reads=[tp], writes=[r_lnf[i]])

        def tr_phase():
            for i in range(NTT):
                tr_tile(i)

        def ffn(dst, rdst, resid, rresid, ln, half_res=None, tile_cb=None):
            tr_tile(0)
            tr_tile(1)
            for i in range(NTT):
                for s2 in range(2):
                    j = (i * 2 + s2) % 4
                    pp, ppt = rps[j], ps[j]
                    p.group("tensor",
                            [lambda e, c=c, i=i, s2=s2, ppt=ppt: e.matmul(
                                ppt[:, :],
                                lnf[:, c, i * 128:(i + 1) * 128],
                                sb["wf"][:, c * D + s2 * 512:c * D + (s2 + 1) * 512],
                                start=(c == 0), stop=(c == 7)) for c in range(8)],
                            reads=[R["wf"], r_lnf[i]], writes=[pp])
                    sl = (i * D + s2 * 512, i * D + (s2 + 1) * 512)
                    p.group("vector",
                            [lambda e, sl=sl, ppt=ppt: e.scalar_tensor_tensor(
                                dst[:, sl[0]:sl[1]], ppt[:, :], 0.5,
                                resid[:, sl[0]:sl[1]], ALU.mult, ALU.add)],
                            reads=[pp, rresid],
                            writes=([rdst] + ([half_res[i // 2]]
                                              if half_res else [])))
                    if ln is not None:
                        ln.feed2(i, s2)
                if i + 2 < NTT:
                    tr_tile(i + 2)
                if tile_cb is not None:
                    tile_cb(i)
            if ln is not None:
                ln.flush()

        LN("a", B[0], rB[0], dma_src=True).standalone()
        for hi, lo, dst in (("dqh", "dql", "dq_c"), ("dkh", "dkl", "dk_c")):
            p.dma(lambda: ct0[:, :], lambda n=hi: dt[n][:, :], writes=[R["ct0"]])
            p.dma(lambda: ct1[:, :], lambda n=lo: dt[n][:, :], writes=[R["ct1"]])
            p.group("vector",
                    [lambda e, d=dst: e.tensor_tensor(sb[d][:, :], ct0[:, :],
                                                      ct1[:, :], ALU.add)],
                    reads=[R["ct0"], R["ct1"]], writes=[R[dst]])
        pump(2)
        cast_next(4)

        for n in range(n_sb):
            p.mark("S%d" % n)
            X, H = B[XI[n % 3]], B[HI[n % 3]]
            rX, rH = rB[XI[n % 3]], rB[HI[n % 3]]
            if 1 <= n and n + 1 < n_sb:
                PB, rPB = B[XI[(n + 1) % 3]], rB[XI[(n + 1) % 3]]
                p.dma(lambda pb=PB: pb[:, :],
                      lambda m=n + 1: dt["xin"][:, m * NTT * D:(m + 1) * NTT * D],
                      writes=[rPB], sem="lx")

            p.mark("trA")
            if n == 0:
                cast_next(4)
            p.mark("ffnA")
            ffn(H, rH, X, rX, LN("bc", H, rH))
            if n == 0:
                cast_next(4)

            p.mark("trB")
            if n == 0:
                cast_next(4)
            p.mark("v")
            tr_tile(0)
            tr_tile(1)
            for i in range(NTT):
                for s2 in range(2):
                    j = (i * 2 + s2) % 4
                    pp, ppt = rps[j], ps[j]
                    p.group("tensor",
                            [lambda e, c=c, i=i, s2=s2, ppt=ppt: e.matmul(
                                ppt[:, :],
                                lnf[:, c, i * 128:(i + 1) * 128],
                                sb["wv"][:, c * D + s2 * 512:c * D + (s2 + 1) * 512],
                                start=(c == 0), stop=(c == 7)) for c in range(8)],
                            reads=[R["wv"], r_lnf[i]], writes=[pp])
                    p.group("scalar",
                            [lambda e, i=i, s2=s2, ppt=ppt: e.activation(
                                v_sb[:, i * D + s2 * 512:i * D + (s2 + 1) * 512],
                                ppt[:, :], AF.Copy)],
                            reads=[pp], writes=[R["v"]])
                if i + 2 < NTT:
                    tr_tile(i + 2)
            p.mark("qk")
            for dst, rdst, wname, bias in ((q_sb, R["q"], "wq", bq_sb),
                                           (kf_sb, R["kf"], "wk", bk_sb)):
                for c2 in range(8):
                    pp, ppt = rps[c2 % 4], ps[c2 % 4]
                    p.group("tensor",
                            [lambda e, c=c, c2=c2, wname=wname, ppt=ppt: e.matmul(
                                ppt[:, :],
                                sb[wname][:, c * D + c2 * 128:c * D + (c2 + 1) * 128],
                                lnf[:, c, :],
                                start=(c == 0), stop=(c == 7)) for c in range(8)],
                            reads=[R[wname]] + r_lnf, writes=[pp])
                    p.group("scalar",
                            [lambda e, c2=c2, dst=dst, bias=bias, ppt=ppt: e.activation(
                                dst[:, c2 * 512:(c2 + 1) * 512], ppt[:, :], AF.Sigmoid,
                                bias=bias[:, c2:c2 + 1])],
                            reads=[pp, R["bq_sb"], R["bk_sb"]], writes=[rdst])
            if n == 0:
                cast_next(4)
                if n_sb > 1:
                    p.dma(lambda: B[2][:, :],
                          lambda: dt["xin"][:, NTT * D:2 * NTT * D],
                          reads=[slot_res[0], slot_res[1]], writes=[rB[2]],
                          sem="lx")
            p.mark("kt")
            for g in range(NTT):
                tp, ptb = rpt[g % 4], pt[g % 4]
                p.group("tensor",
                        [lambda e, c2=c2, g=g, ptb=ptb: e.transpose(
                            ptb[:, c2 * 128:(c2 + 1) * 128],
                            kf_sb[:, c2 * 512 + g * 128:c2 * 512 + (g + 1) * 128],
                            id_sb[:, :]) for c2 in range(8)],
                        reads=[R["kf"], R["id_sb"]], writes=[tp])
                col = n * NTT + g
                p.group("scalar",
                        [lambda e, g=g, h2=h2, ptb=ptb, col=col: e.activation(
                            lnf[:, 2 * g + h2, :], ptb[:, h2 * 512:(h2 + 1) * 512],
                            AF.Identity, scale=dk_c[:, col:col + 1])
                         for h2 in range(2)],
                        reads=[tp, R["dk_c"]], writes=[r_lnf[g]])
            p.mark("sc")
            for j in range(NTT):
                pp, ppt = rps[j % 4], ps[j % 4]
                p.group("tensor",
                        [lambda e, c2=c2, j=j, ppt=ppt: e.matmul(
                            ppt[:, j * 128:512],
                            kf_sb[:, c2 * 512 + j * 128:c2 * 512 + (j + 1) * 128],
                            q_sb[:, c2 * 512 + j * 128:(c2 + 1) * 512],
                            start=(c2 == 0), stop=(c2 == 7)) for c2 in range(8)],
                        reads=[R["kf"], R["q"]], writes=[pp])
                col = n * NTT + j
                fns = []
                fns.append(lambda e, j=j, col=col, ppt=ppt: e.scalar_tensor_tensor(
                    sc_sb[:, j * 512 + j * 128:j * 512 + (j + 1) * 128],
                    ppt[:, j * 128:(j + 1) * 128], dk_c[:, col:col + 1],
                    tril_sb[:, :], ALU.mult, ALU.mult))
                if j < 3:
                    fns.append(lambda e, j=j, col=col, ppt=ppt: e.tensor_scalar(
                        sc_sb[:, j * 512 + (j + 1) * 128:(j + 1) * 512],
                        ppt[:, (j + 1) * 128:512], dk_c[:, col:col + 1], None,
                        ALU.mult))
                p.group("vector", fns, reads=[pp, R["dk_c"], R["tril_sb"]],
                        writes=[R["sc"]])
            p.mark("y")
            for m in range(8):
                pp, ppt = rps[m % 4], ps[m % 4]
                fns = [lambda e, jj=jj, m=m, ppt=ppt, n=n: e.matmul(
                    ppt[:, jj * 128:512],
                    v_sb[:, jj * D + m * 128:jj * D + (m + 1) * 128],
                    sc_sb[:, jj * 512 + jj * 128:(jj + 1) * 512],
                    start=(jj == 0), stop=(jj == 3 and n == 0)) for jj in range(NTT)]
                if n > 0:
                    fns += [lambda e, c2=c2, m=m, ppt=ppt: e.matmul(
                        ppt[:, :], S_sb[:, c2 * D + m * 128:c2 * D + (m + 1) * 128],
                        q_sb[:, c2 * 512:(c2 + 1) * 512],
                        start=False, stop=(c2 == 7)) for c2 in range(8)]
                p.group("tensor", fns, reads=[R["v"], R["sc"], R["S"], R["q"]],
                        writes=[pp])
                p.group("vector",
                        [lambda e, m=m, ppt=ppt: e.tensor_copy(
                            ya_sb[:, m * 512:(m + 1) * 512], ppt[:, :])],
                        reads=[pp], writes=[R["ya"]])
            p.mark("kv")
            if n + 1 < n_sb:
                for mk in range(8):
                    for s2 in range(2):
                        j2 = (mk * 2 + s2) % 4
                        pp, ppt = rps[j2], ps[j2]
                        p.group("tensor",
                                [lambda e, jj=jj, mk=mk, s2=s2, ppt=ppt: e.matmul(
                                    ppt[:, :],
                                    lnf[:, 2 * jj + mk // 4,
                                        (mk % 4) * 128:(mk % 4 + 1) * 128],
                                    v_sb[:, jj * D + s2 * 512:jj * D + (s2 + 1) * 512],
                                    start=(jj == 0), stop=(jj == 3))
                                 for jj in range(NTT)],
                                reads=r_lnf + [R["v"]], writes=[pp])
                        sl = (mk * D + s2 * 512, mk * D + (s2 + 1) * 512)
                        if n == 0:
                            p.group("vector",
                                    [lambda e, sl=sl, ppt=ppt: e.tensor_copy(
                                        S_sb[:, sl[0]:sl[1]], ppt[:, :])],
                                    reads=[pp], writes=[R["S"]])
                        else:
                            p.group("vector",
                                    [lambda e, sl=sl, ppt=ppt: e.tensor_tensor(
                                        S_sb[:, sl[0]:sl[1]], ppt[:, :],
                                        S_sb[:, sl[0]:sl[1]], ALU.add)],
                                    reads=[pp], writes=[R["S"]])
            p.mark("wo")
            ln_c = LN("bc", X, rX)
            for i in range(NTT):
                for s2 in range(2):
                    j2 = (i * 2 + s2) % 4
                    pp, ppt = rps[j2], ps[j2]
                    p.group("tensor",
                            [lambda e, c2=c2, i=i, s2=s2, ppt=ppt: e.matmul(
                                ppt[:, :],
                                ya_sb[:, c2 * 512 + i * 128:c2 * 512 + (i + 1) * 128],
                                sb["wo"][:, c2 * D + s2 * 512:c2 * D + (s2 + 1) * 512],
                                start=(c2 == 0), stop=(c2 == 7)) for c2 in range(8)],
                            reads=[R["ya"], R["wo"]], writes=[pp])
                    col = n * NTT + i
                    sl = (i * D + s2 * 512, i * D + (s2 + 1) * 512)
                    p.group("vector",
                            [lambda e, sl=sl, col=col, ppt=ppt, X=X, H=H:
                             e.scalar_tensor_tensor(
                                X[:, sl[0]:sl[1]], ppt[:, :], dq_c[:, col:col + 1],
                                H[:, sl[0]:sl[1]], ALU.mult, ALU.add)],
                            reads=[pp, rH, R["dq_c"]], writes=[rX])
                    ln_c.feed2(i, s2)
            ln_c.flush()

            p.mark("trC")
            p.mark("ffnC")
            if n + 1 < n_sb:
                PB, rPB = B[XI[(n + 1) % 3]], rB[XI[(n + 1) % 3]]
                ln_a = LN("a", PB, rPB, dma_src=True)
            else:
                ln_a = None
            rH2 = [Res("H%d_h%d" % (n, h)) for h in range(2)]

            def store_cb(i, n=n, H=H, rH2=rH2):
                if i in (1, 3):
                    h = i // 2
                    p.dma(lambda n=n, h=h: out_d[:, n * NTT * D + h * 2 * D:
                                                 n * NTT * D + (h + 1) * 2 * D],
                          lambda H=H, h=h: H[:, h * 2 * D:(h + 1) * 2 * D],
                          reads=[rH2[h]], sem="st")
            ffn(H, rH, X, rX, ln_a, half_res=rH2, tile_cb=store_cb)

        p.items["sync"].append(("wait", "st", p.cnt["st"]))
        LAST_PLAN = p

        with nc.Block() as block:
            for eng in ("sync", "tensor", "vector", "scalar", "gpsimd"):
                items = p.items[eng]
                if not items:
                    continue

                def body(e, items=items):
                    for it in items:
                        if it[0] == "wait":
                            e.wait_ge(S[it[1]], it[2])
                        else:
                            inst = it[1](e)
                            if it[2] is not None:
                                inst.then_inc(S[it[2][0]], it[2][1])
                getattr(block, eng)(body)
    return nc


def _tile128(w):
    n = w.shape[1]
    return np.ascontiguousarray(
        w.reshape(8, 128, n).transpose(1, 0, 2).reshape(128, 8 * n))


def _hilo(vals64):
    c = np.ascontiguousarray(vals64.reshape(32, 128).T)
    hi = c.astype(np.float32)
    lo = (c - hi.astype(np.float64)).astype(np.float32)
    return hi, lo


def kernel(L0_attn_x, L0_attn_Wv, L0_attn_Wk, L0_attn_bk, L0_attn_Wq,
           L0_attn_bq, L0_attn_Wo, L0_ffn_weight, L0_ffn_disha):
    bf16 = ml_dtypes.bfloat16
    x = np.asarray(L0_attn_x, dtype=np.float32)
    wq = _tile128(np.asarray(L0_attn_Wq, dtype=np.float32))
    wk = _tile128(np.asarray(L0_attn_Wk, dtype=np.float32))
    wv = _tile128(np.asarray(L0_attn_Wv, dtype=np.float32))
    wo = _tile128(np.asarray(L0_attn_Wo, dtype=np.float32))
    wt = _tile128(np.ascontiguousarray(np.asarray(L0_ffn_weight, dtype=np.float32).T))
    disha = np.asarray(L0_ffn_disha, dtype=np.float32)
    disha2 = np.ascontiguousarray(np.concatenate([disha, disha], axis=0))
    bq = np.ascontiguousarray(np.asarray(L0_attn_bq, np.float32).reshape(8, 128).T)
    bk = np.ascontiguousarray(np.asarray(L0_attn_bk, np.float32).reshape(8, 128).T)
    s64 = np.arange(T, dtype=np.float64)
    dqh, dql = _hilo(np.float64(DECAY) ** s64)
    dkh, dkl = _hilo(np.float64(DECAY) ** (-s64))
    tril = np.triu(np.ones((128, 128))).astype(bf16)
    ident = np.eye(128).astype(bf16)

    nc = build_nc()
    in_maps = []
    for c in range(8):
        xc = np.ascontiguousarray(
            x[c].reshape(32, 128, D).transpose(1, 0, 2).reshape(128, 32 * D))
        in_maps.append({"xin": xc, "wq": wq, "wk": wk, "wv": wv, "wo": wo,
                        "wt": wt, "disha2": disha2, "bq": bq, "bk": bk,
                        "dqh": dqh, "dql": dql, "dkh": dkh, "dkl": dkl,
                        "tril": tril, "ident": ident})
    res = run_bass_kernel_spmd(nc, in_maps, core_ids=list(range(8)))
    outs = []
    for r in res.results:
        o = np.asarray(r["out"]).reshape(128, 32, D).transpose(1, 0, 2)
        outs.append(o.reshape(T, D))
    return np.stack(outs).astype(np.float32)


if __name__ == "__main__":
    import reference
    ins = reference.setup_inputs()
    out = kernel(**{k: np.asarray(v) for k, v in ins.items()})
    ref = np.asarray(reference.reference(**ins))
    err = np.linalg.norm(out - ref) / np.linalg.norm(ref)
    print("rel_err", err)


# version 30
# speedup vs baseline: 1.0180x; 1.0048x over previous
import contextlib
import numpy as np
import ml_dtypes
import concourse.bass as bass
import concourse.mybir as mybir
from concourse.bass_utils import run_bass_kernel_spmd

F32 = mybir.dt.float32
BF16 = mybir.dt.bfloat16
AF = mybir.ActivationFunctionType
ALU = mybir.AluOpType

T, D = 4096, 1024
NSB, SBT, NTT = 8, 512, 4
DECAY = 0.99
EPS = 1e-5
APP_SPLIT = False
TR_COPY_SPLIT = False

SEMS = ["pe", "dv", "ac", "gp", "lx", "gx", "st", "ld0", "ld1", "ld2", "ld3"]
ENG_OF_SEM = {"pe": "tensor", "dv": "vector", "ac": "scalar", "gp": "gpsimd",
              "lx": "sync", "gx": "sync", "st": "sync", "ld0": "sync",
              "ld1": "sync", "ld2": "sync", "ld3": "sync"}
ENG_SEM = {"tensor": "pe", "vector": "dv", "scalar": "ac", "gpsimd": "gp"}
COMPUTE_SEMS = ("pe", "dv", "ac", "gp")


class Res:
    __slots__ = ("name", "w", "r")

    def __init__(self, name):
        self.name, self.w, self.r = name, None, {}


class Plan:
    def __init__(self):
        self.items = {e: [] for e in ("tensor", "vector", "scalar", "sync", "gpsimd")}
        self.cnt = {s: 0 for s in SEMS}
        self.floor = {e: {} for e in self.items}
        self.ld_rr = 0
        self.marks = {e: [] for e in self.items}
        self.ncall = {e: 0 for e in self.items}

    def mark(self, label):
        for e in self.marks:
            self.marks[e].append((len(self.items[e]), label))

    def group(self, eng, fns, reads=(), writes=(), sem=None, dma=False, record=()):
        if sem is None:
            sem = ENG_SEM[eng]
        amt = 16 if dma else 1
        waits = {}

        def need(s, v):
            if s in COMPUTE_SEMS and ENG_OF_SEM[s] == eng:
                return
            if v > waits.get(s, 0):
                waits[s] = v

        ncalls = self.ncall[eng]
        for r in reads:
            if r.w is not None:
                need(r.w[0], r.w[1])
                if r.w[2] == eng and eng != "sync" and ncalls - r.w[3] < 2:
                    raise AssertionError(
                        "same-engine RAW too close: %s dist=%d eng=%s"
                        % (r.name, ncalls - r.w[3], eng))
        for r in writes:
            if r.w is not None:
                need(r.w[0], r.w[1])
            for s, v in r.r.items():
                need(s, v)
        if dma:
            if self.cnt[sem] > 0:
                need(sem, self.cnt[sem])
        fl = self.floor[eng]
        for s in sorted(waits):
            v = waits[s]
            if fl.get(s, 0) >= v:
                continue
            self.items[eng].append(("wait", s, v))
            fl[s] = v
        self.cnt[sem] += amt
        val = self.cnt[sem]
        for i, fn in enumerate(fns):
            self.items[eng].append(
                ("call", fn, (sem, amt) if i == len(fns) - 1 else None))
        self.ncall[eng] += len(fns)
        for r in writes:
            r.w = (sem, val, eng, self.ncall[eng])
            r.r = {}
        for r in reads:
            if val > r.r.get(sem, 0):
                r.r[sem] = val
        for r in record:
            if val > r.r.get(sem, 0):
                r.r[sem] = val
        return (sem, val)

    def dma(self, out_fn, in_fn, reads=(), writes=(), sem=None, eng="sync"):
        if sem is None:
            sem = "ld%d" % (self.ld_rr % 4)
            self.ld_rr += 1
        return self.group(eng, [lambda e: e.dma_start(out=out_fn(), in_=in_fn())],
                          reads=reads, writes=writes, sem=sem, dma=True)


LAST_PLAN = None


def build_nc(n_sb=NSB):
    global LAST_PLAN
    nc = bass.Bass("TRN2", target_bir_lowering=False, detect_race_conditions=False)
    dt = {}

    def din(name, shape, dty=F32):
        dt[name] = nc.dram_tensor(name, shape, dty, kind="ExternalInput").ap()

    din("xin", [128, T * D // 128])
    for w in ("wq", "wk", "wv", "wo", "wt"):
        din(w, [128, 8 * D])
    din("disha2", [128, D])
    din("bq", [128, 8])
    din("bk", [128, 8])
    for nm in ("dqh", "dql", "dkh", "dkl"):
        din(nm, [128, 32])
    din("tril", [128, 128], BF16)
    din("ident", [128, 128], BF16)
    out_d = nc.dram_tensor("out", [128, T * D // 128], F32, kind="ExternalOutput").ap()

    p = Plan()
    with contextlib.ExitStack() as ctx:
        S = {s: ctx.enter_context(nc.semaphore(s)) for s in SEMS}
        sb = {}

        def sbuf(name, shape, dty):
            sb[name] = ctx.enter_context(nc.sbuf_tensor("s_" + name, shape, dty))
            return sb[name]

        for w in ("wq", "wk", "wv", "wo", "wf"):
            sbuf(w, [128, 8 * D], BF16)
        S_sb = sbuf("S", [128, 8 * D], BF16)
        B = [sbuf("B%d" % i, [128, NTT * D], F32) for i in range(3)]
        ln_tm = sbuf("ln_tm", [128, NTT * D], BF16)
        lnf = sbuf("lnf", [128, 8, SBT], BF16)
        q_sb = sbuf("q", [128, 8 * SBT], BF16)
        kf_sb = sbuf("kf", [128, 8 * SBT], BF16)
        v_sb = sbuf("v", [128, NTT * D], BF16)
        sc_sb = sbuf("sc", [128, NTT * SBT], BF16)
        ya_sb = sbuf("ya", [128, 8 * SBT], BF16)
        dq_c = sbuf("dq_c", [128, 32], F32)
        dk_c = sbuf("dk_c", [128, 32], F32)
        ct0 = sbuf("ct0", [128, 32], F32)
        ct1 = sbuf("ct1", [128, 32], F32)
        bq_sb = sbuf("bq_sb", [128, 8], F32)
        bk_sb = sbuf("bk_sb", [128, 8], F32)
        tril_sb = sbuf("tril_sb", [128, 128], BF16)
        id_sb = sbuf("id_sb", [128, 128], BF16)
        dsh_sb = sbuf("dsh", [128, D], F32)
        ec = sbuf("ec", [128, 4], F32)
        stat = {tag: {nm: sbuf(tag + "_" + nm, [128, w], F32)
                      for nm, w in (("stats", 48), ("mv", 8), ("negm", 16),
                                    ("vpe", 4), ("rstd", 64), ("negms", 16))}
                for tag in ("a", "bc")}
        scr = sbuf("scr", [128, 4], F32)
        scrA = sbuf("scrA", [128, 4], F32)
        wide = sbuf("wide", [128, 512], F32)
        neghalf = sbuf("neghalf", [128, 4], F32)

        ps = [ctx.enter_context(nc.psum_tensor("p%d" % i, [128, 512], F32))
              for i in range(4)]
        pt = [ctx.enter_context(nc.psum_tensor("t%d" % i, [128, 1024], BF16))
              for i in range(4)]

        R = {n: Res(n) for n in
             list(sb) + ["p%d" % i for i in range(4)] + ["t%d" % i for i in range(4)]}
        rB = [R["B0"], R["B1"], R["B2"]]
        rps = [R["p%d" % i] for i in range(4)]
        rpt = [R["t%d" % i] for i in range(4)]
        r_ln_a = [Res("lna_%d" % i) for i in range(NTT)]
        r_ln_d = [Res("lnd_%d" % i) for i in range(NTT)]
        r_lnf = [Res("lnf_%d" % i) for i in range(NTT)]
        rstat = {tag: {nm: [Res(tag + nm + str(i)) for i in range(NTT)]
                       for nm in ("stats", "mv", "negm", "vpe", "rstd",
                                  "negms")}
                 for tag in ("a", "bc")}

        p.dma(lambda: B[0][:, :], lambda: dt["xin"][:, 0:NTT * D],
              writes=[rB[0]], sem="lx")
        for name, dest in (("ident", id_sb), ("tril", tril_sb), ("bq", bq_sb),
                           ("bk", bk_sb), ("disha2", dsh_sb)):
            p.dma(lambda d=dest: d[:, :], lambda n=name: dt[n][:, :],
                  writes=[R[{"ident": "id_sb", "tril": "tril_sb", "bq": "bq_sb",
                             "bk": "bk_sb", "disha2": "dsh"}[name]]])
        p.group("vector", [lambda e: e.memset(ec[:, :], EPS)], writes=[R["ec"]])
        p.group("vector", [lambda e: e.memset(neghalf[:, :], -0.5)],
                writes=[R["neghalf"]])
        p.group("vector", [lambda e: e.memset(wide[:, :], 0.0)],
                writes=[R["wide"]])

        slot_res = [Res("wslot%d" % i) for i in range(2)]

        def slot_ap(i):
            return B[2][:, i * 2048:(i + 1) * 2048]
        slot_rb = [rB[2], rB[2]]
        w_order = ("wt", "wq", "wk", "wv", "wo")
        w_scale = {"wq": 1.0, "wk": 1.0, "wv": 0.01, "wo": 1.0}
        w_eng = {"wq": "scalar", "wk": "scalar", "wv": "scalar", "wo": "vector"}

        LOADS = [(wi, q) for wi in range(5) for q in range(4)]
        dma_ptr = [0]
        cast_ptr = [0]

        def pump(k):
            for _ in range(k):
                if dma_ptr[0] >= len(LOADS):
                    return
                wi, q = LOADS[dma_ptr[0]]
                dma_ptr[0] += 1
                sl = (wi * 4 + q) % 2
                p.dma(lambda sl=sl: slot_ap(sl),
                      lambda w=w_order[wi], q=q: dt[w][:, q * 2048:(q + 1) * 2048],
                      writes=[slot_res[sl]])

        def cast_next(m):
            for _ in range(m):
                wi, q = LOADS[cast_ptr[0]]
                cast_ptr[0] += 1
                sl = (wi * 4 + q) % 2
                w = w_order[wi]
                if w == "wt":
                    p.group("vector",
                            [lambda e, sl=sl, q=q, cc=cc: e.scalar_tensor_tensor(
                                sb["wf"][:, q * 2048 + cc * D:q * 2048 + (cc + 1) * D],
                                slot_ap(sl)[:, cc * D:(cc + 1) * D], 1.0,
                                dsh_sb[:, :], ALU.mult, ALU.add)
                             for cc in range(2)],
                            reads=[slot_res[sl], R["dsh"]], record=[rB[2]],
                            writes=[R["wf"]])
                elif w_eng[w] == "scalar":
                    p.group("scalar",
                            [lambda e, sl=sl, w=w, q=q: e.activation(
                                sb[w][:, q * 2048:(q + 1) * 2048], slot_ap(sl),
                                AF.Copy, scale=w_scale[w])],
                            reads=[slot_res[sl]], record=[rB[2]], writes=[R[w]])
                else:
                    p.group("vector",
                            [lambda e, sl=sl, w=w, q=q: e.tensor_scalar(
                                sb[w][:, q * 2048:(q + 1) * 2048], slot_ap(sl),
                                w_scale[w], None, ALU.mult)],
                            reads=[slot_res[sl]], record=[rB[2]], writes=[R[w]])
                pump(1)

        XI = [0, 2, 1]
        HI = [1, 0, 2]

        class LN:

            def __init__(self, tag, xbuf, rx, dma_src=False, rx_tiles=None):
                self.st, self.rr = stat[tag], rstat[tag]
                self.x, self.rx, self.dma = xbuf, rx, dma_src
                self.rxt = rx_tiles

            def s1a(self, i):
                st, x = self.st, self.x
                p.group("vector",
                        [lambda e: e.bn_stats(st["stats"][:, 12 * i:12 * i + 6],
                                              x[:, i * D:i * D + 512])],
                        reads=([self.rxt[i]] if self.rxt else
                               ([self.rx] if self.dma else [])),
                        writes=[self.rr["stats"][i]])

            def s1b(self, i):
                st, x = self.st, self.x
                p.group("vector",
                        [lambda e: e.bn_stats(st["stats"][:, 12 * i + 6:12 * i + 12],
                                              x[:, i * D + 512:(i + 1) * D])],
                        reads=([self.rxt[i]] if self.rxt else
                               ([self.rx] if self.dma else [])),
                        writes=[self.rr["stats"][i]])

            def s2(self, i):
                st = self.st
                p.group("vector",
                        [lambda e: e.bn_aggr(st["mv"][:, 2 * i:2 * i + 2],
                                             st["stats"][:, 12 * i:12 * i + 12])],
                        reads=[self.rr["stats"][i]], writes=[self.rr["mv"][i]])

            def s3(self, i):
                st = self.st
                p.group("vector",
                        [lambda e: e.tensor_scalar(st["negm"][:, 4 * i:4 * i + 1],
                                                   st["mv"][:, 2 * i:2 * i + 1],
                                                   -1.0, None, ALU.mult),
                         lambda e: e.tensor_scalar(st["vpe"][:, i:i + 1],
                                                   st["mv"][:, 2 * i + 1:2 * i + 2],
                                                   EPS, None, ALU.add)],
                        reads=[self.rr["mv"][i]],
                        writes=[self.rr["negm"][i], self.rr["vpe"][i]])

            def pw(self, i):
                st = self.st
                p.group("gpsimd",
                        [lambda e: e.tensor_tensor(st["rstd"][:, 16 * i:16 * i + 1],
                                                   st["vpe"][:, i:i + 1],
                                                   neghalf[:, 0:1], ALU.pow)],
                        reads=[self.rr["vpe"][i], R["neghalf"]],
                        writes=[self.rr["rstd"][i]])

            def ngs(self, i):
                st = self.st
                p.group("scalar",
                        [lambda e: e.activation(st["negms"][:, 4 * i:4 * i + 1],
                                                st["negm"][:, 4 * i:4 * i + 1], AF.Identity,
                                                scale=st["rstd"][:, 16 * i:16 * i + 1])],
                        reads=[self.rr["negm"][i], self.rr["rstd"][i]],
                        writes=[self.rr["negms"][i]])

            def app_a(self, i):
                st, x = self.st, self.x
                p.group("scalar",
                        [lambda e: e.activation(
                            ln_tm[:, i * D:i * D + 512],
                            x[:, i * D:i * D + 512],
                            AF.Identity, bias=st["negms"][:, 4 * i:4 * i + 1],
                            scale=st["rstd"][:, 16 * i:16 * i + 1])],
                        reads=[self.rr["negms"][i], self.rr["rstd"][i]],
                        record=[self.rx], writes=[r_ln_a[i]])

            def app_d(self, i):
                st, x = self.st, self.x
                if APP_SPLIT:
                    p.group("vector",
                            [lambda e: e.tensor_scalar(
                                ln_tm[:, i * D + 512:(i + 1) * D],
                                x[:, i * D + 512:(i + 1) * D],
                                st["negm"][:, 4 * i:4 * i + 1], st["rstd"][:, 4 * i:4 * i + 1],
                                ALU.add, ALU.mult)],
                            reads=[self.rr["negm"][i], self.rr["rstd"][i]],
                            record=[self.rx], writes=[r_ln_d[i]])
                else:
                    p.group("scalar",
                            [lambda e: e.activation(
                                ln_tm[:, i * D + 512:(i + 1) * D],
                                x[:, i * D + 512:(i + 1) * D],
                                AF.Identity, bias=st["negms"][:, 4 * i:4 * i + 1],
                                scale=st["rstd"][:, 16 * i:16 * i + 1])],
                            reads=[self.rr["negms"][i], self.rr["rstd"][i]],
                            record=[self.rx], writes=[r_ln_d[i]])

            def pad(self):
                p.group("vector", [lambda e: e.memset(scr[:, 0:1], 0.0)])

            def acpad(self):
                p.group("scalar",
                        [lambda e: e.activation(scrA[:, 0:1], ec[:, 0:1],
                                                AF.Copy)],
                        reads=[R["ec"]])

            def pad_w(self):
                p.group("vector",
                        [lambda e: e.tensor_scalar(wide[:, 256:512],
                                                   wide[:, 0:256], 1.0, None,
                                                   ALU.mult)],
                        reads=[R["wide"]])

            def acpad_w(self):
                p.group("scalar",
                        [lambda e: e.activation(wide[:, 256:512],
                                                wide[:, 0:256], AF.Copy)],
                        reads=[R["wide"]])

            def feed2(self, i, s2):
                if (i, s2) == (0, 1):
                    self.pad(); self.pad(); self.s1a(0); self.s1b(0)
                elif (i, s2) == (1, 0):
                    self.pad(); self.s2(0)
                elif (i, s2) == (1, 1):
                    self.pad(); self.s3(0); self.s1a(1); self.s1b(1)
                    self.pw(0); self.ngs(0)
                elif (i, s2) == (2, 0):
                    self.pad(); self.s2(1)
                elif (i, s2) == (2, 1):
                    self.pad(); self.s3(1); self.s1a(2); self.s1b(2)
                    self.pw(1); self.ngs(1); self.acpad()
                    self.app_a(0); self.app_d(0)
                elif (i, s2) == (3, 0):
                    self.pad(); self.s2(2)
                elif (i, s2) == (3, 1):
                    self.pad(); self.s3(2); self.s1a(3); self.s1b(3)
                    self.pw(2); self.ngs(2); self.app_a(1); self.app_d(1)

            def flush(self):
                self.pad_w(); self.pad_w(); self.s2(3)
                self.app_a(2); self.app_d(2)
                self.pad_w(); self.pad_w(); self.s3(3); self.pw(3)
                self.ngs(3); self.acpad_w(); self.acpad_w()
                self.app_a(3); self.app_d(3)

            def standalone(self):
                self.s1a(0); self.s1b(0); self.s1a(1); self.s1b(1)
                self.s2(0); self.s1a(2); self.s1b(2); self.s2(1)
                self.s3(0); self.pw(0); self.s1a(3); self.s1b(3)
                self.s2(2); self.s3(1); self.pw(1); self.ngs(0)
                self.s2(3); self.s3(2); self.pw(2); self.ngs(1)
                self.acpad(); self.app_a(0); self.app_d(0)
                self.s3(3); self.pw(3); self.ngs(2)
                self.app_a(1); self.app_d(1); self.ngs(3)
                self.app_a(2); self.app_d(2); self.app_a(3); self.app_d(3)

        def tr_tile(i):
            if True:
                tp, ptb = rpt[i % 4], pt[i % 4]
                p.group("tensor",
                        [lambda e, c=c, i=i, ptb=ptb: e.transpose(
                            ptb[:, c * 128:(c + 1) * 128],
                            ln_tm[:, i * D + c * 128:i * D + (c + 1) * 128],
                            id_sb[:, :]) for c in range(8)],
                        reads=[r_ln_a[i], r_ln_d[i], R["id_sb"]], writes=[tp])
                p.group("vector",
                        [lambda e, i=i, ptb=ptb: e.tensor_copy(
                            lnf[:, :, i * 128:(i + 1) * 128], ptb[:, :])],
                        reads=[tp], writes=[r_lnf[i]])

        def tr_phase():
            for i in range(NTT):
                tr_tile(i)

        def ffn(dst, rdst, resid, rresid, ln, half_res=None, tile_cb=None,
                skip_tr01=False, no_flush=False):
            if not skip_tr01:
                tr_tile(0)
                tr_tile(1)
            for i in range(NTT):
                for s2 in range(2):
                    j = (i * 2 + s2) % 4
                    pp, ppt = rps[j], ps[j]
                    p.group("tensor",
                            [lambda e, c=c, i=i, s2=s2, ppt=ppt: e.matmul(
                                ppt[:, :],
                                lnf[:, c, i * 128:(i + 1) * 128],
                                sb["wf"][:, c * D + s2 * 512:c * D + (s2 + 1) * 512],
                                start=(c == 0), stop=(c == 7)) for c in range(8)],
                            reads=[R["wf"], r_lnf[i]], writes=[pp])
                    sl = (i * D + s2 * 512, i * D + (s2 + 1) * 512)
                    p.group("vector",
                            [lambda e, sl=sl, ppt=ppt: e.scalar_tensor_tensor(
                                dst[:, sl[0]:sl[1]], ppt[:, :], 0.5,
                                resid[:, sl[0]:sl[1]], ALU.mult, ALU.add)],
                            reads=[pp, rresid],
                            writes=([rdst] + ([half_res[i // 2]]
                                              if half_res else [])))
                    if ln is not None:
                        ln.feed2(i, s2)
                if i + 2 < NTT:
                    tr_tile(i + 2)
                if tile_cb is not None:
                    tile_cb(i)
            if ln is not None and not no_flush:
                ln.flush()

        LN("a", B[0], rB[0], dma_src=True).standalone()
        for hi, lo, dst in (("dqh", "dql", "dq_c"), ("dkh", "dkl", "dk_c")):
            p.dma(lambda: ct0[:, :], lambda n=hi: dt[n][:, :], writes=[R["ct0"]])
            p.dma(lambda: ct1[:, :], lambda n=lo: dt[n][:, :], writes=[R["ct1"]])
            p.group("vector",
                    [lambda e, d=dst: e.tensor_tensor(sb[d][:, :], ct0[:, :],
                                                      ct1[:, :], ALU.add)],
                    reads=[R["ct0"], R["ct1"]], writes=[R[dst]])
        pump(2)
        cast_next(4)

        for n in range(n_sb):
            p.mark("S%d" % n)
            X, H = B[XI[n % 3]], B[HI[n % 3]]
            rX, rH = rB[XI[n % 3]], rB[HI[n % 3]]
            if 1 <= n and n + 1 < n_sb:
                PB, rPB = B[XI[(n + 1) % 3]], rB[XI[(n + 1) % 3]]
                p.dma(lambda pb=PB: pb[:, :],
                      lambda m=n + 1: dt["xin"][:, m * NTT * D:(m + 1) * NTT * D],
                      writes=[rPB], sem="lx")

            p.mark("trA")
            if n == 0:
                cast_next(4)
            p.mark("ffnA")
            ln_b = LN("bc", H, rH)
            ffn(H, rH, X, rX, ln_b, skip_tr01=(n > 0), no_flush=True)
            if n == 0:
                cast_next(4)

            p.mark("trB")
            if n == 0:
                cast_next(4)
            p.mark("v")
            tr_tile(0)
            tr_tile(1)
            ln_b.flush()
            for i in range(NTT):
                for s2 in range(2):
                    j = (i * 2 + s2) % 4
                    pp, ppt = rps[j], ps[j]
                    p.group("tensor",
                            [lambda e, c=c, i=i, s2=s2, ppt=ppt: e.matmul(
                                ppt[:, :],
                                lnf[:, c, i * 128:(i + 1) * 128],
                                sb["wv"][:, c * D + s2 * 512:c * D + (s2 + 1) * 512],
                                start=(c == 0), stop=(c == 7)) for c in range(8)],
                            reads=[R["wv"], r_lnf[i]], writes=[pp])
                    p.group("scalar",
                            [lambda e, i=i, s2=s2, ppt=ppt: e.activation(
                                v_sb[:, i * D + s2 * 512:i * D + (s2 + 1) * 512],
                                ppt[:, :], AF.Copy)],
                            reads=[pp], writes=[R["v"]])
                if i + 2 < NTT:
                    tr_tile(i + 2)
            p.mark("qk")
            for dst, rdst, wname, bias in ((q_sb, R["q"], "wq", bq_sb),
                                           (kf_sb, R["kf"], "wk", bk_sb)):
                for c2 in range(8):
                    pp, ppt = rps[c2 % 4], ps[c2 % 4]
                    p.group("tensor",
                            [lambda e, c=c, c2=c2, wname=wname, ppt=ppt: e.matmul(
                                ppt[:, :],
                                sb[wname][:, c * D + c2 * 128:c * D + (c2 + 1) * 128],
                                lnf[:, c, :],
                                start=(c == 0), stop=(c == 7)) for c in range(8)],
                            reads=[R[wname]] + r_lnf, writes=[pp])
                    p.group("scalar",
                            [lambda e, c2=c2, dst=dst, bias=bias, ppt=ppt: e.activation(
                                dst[:, c2 * 512:(c2 + 1) * 512], ppt[:, :], AF.Sigmoid,
                                bias=bias[:, c2:c2 + 1])],
                            reads=[pp, R["bq_sb"], R["bk_sb"]], writes=[rdst])
            if n == 0:
                cast_next(4)
                if n_sb > 1:
                    p.dma(lambda: B[2][:, :],
                          lambda: dt["xin"][:, NTT * D:2 * NTT * D],
                          reads=[slot_res[0], slot_res[1]], writes=[rB[2]],
                          sem="lx")
            p.mark("kt")
            for g in range(NTT):
                tp, ptb = rpt[g % 4], pt[g % 4]
                p.group("tensor",
                        [lambda e, c2=c2, g=g, ptb=ptb: e.transpose(
                            ptb[:, c2 * 128:(c2 + 1) * 128],
                            kf_sb[:, c2 * 512 + g * 128:c2 * 512 + (g + 1) * 128],
                            id_sb[:, :]) for c2 in range(8)],
                        reads=[R["kf"], R["id_sb"]], writes=[tp])
                col = n * NTT + g
                p.group("scalar",
                        [lambda e, g=g, h2=h2, ptb=ptb, col=col: e.activation(
                            lnf[:, 2 * g + h2, :], ptb[:, h2 * 512:(h2 + 1) * 512],
                            AF.Identity, scale=dk_c[:, col:col + 1])
                         for h2 in range(2)],
                        reads=[tp, R["dk_c"]], writes=[r_lnf[g]])
            p.mark("sc")
            for j in range(NTT):
                pp, ppt = rps[j % 4], ps[j % 4]
                p.group("tensor",
                        [lambda e, c2=c2, j=j, ppt=ppt: e.matmul(
                            ppt[:, j * 128:512],
                            kf_sb[:, c2 * 512 + j * 128:c2 * 512 + (j + 1) * 128],
                            q_sb[:, c2 * 512 + j * 128:(c2 + 1) * 512],
                            start=(c2 == 0), stop=(c2 == 7)) for c2 in range(8)],
                        reads=[R["kf"], R["q"]], writes=[pp])
                col = n * NTT + j
                fns = []
                fns.append(lambda e, j=j, col=col, ppt=ppt: e.scalar_tensor_tensor(
                    sc_sb[:, j * 512 + j * 128:j * 512 + (j + 1) * 128],
                    ppt[:, j * 128:(j + 1) * 128], dk_c[:, col:col + 1],
                    tril_sb[:, :], ALU.mult, ALU.mult))
                if j < 3:
                    fns.append(lambda e, j=j, col=col, ppt=ppt: e.tensor_scalar(
                        sc_sb[:, j * 512 + (j + 1) * 128:(j + 1) * 512],
                        ppt[:, (j + 1) * 128:512], dk_c[:, col:col + 1], None,
                        ALU.mult))
                p.group("vector", fns, reads=[pp, R["dk_c"], R["tril_sb"]],
                        writes=[R["sc"]])
            p.mark("y")
            for m in range(8):
                pp, ppt = rps[m % 4], ps[m % 4]
                fns = [lambda e, jj=jj, m=m, ppt=ppt, n=n: e.matmul(
                    ppt[:, jj * 128:512],
                    v_sb[:, jj * D + m * 128:jj * D + (m + 1) * 128],
                    sc_sb[:, jj * 512 + jj * 128:(jj + 1) * 512],
                    start=(jj == 0), stop=(jj == 3 and n == 0)) for jj in range(NTT)]
                if n > 0:
                    fns += [lambda e, c2=c2, m=m, ppt=ppt: e.matmul(
                        ppt[:, :], S_sb[:, c2 * D + m * 128:c2 * D + (m + 1) * 128],
                        q_sb[:, c2 * 512:(c2 + 1) * 512],
                        start=False, stop=(c2 == 7)) for c2 in range(8)]
                p.group("tensor", fns, reads=[R["v"], R["sc"], R["S"], R["q"]],
                        writes=[pp])
                p.group("vector",
                        [lambda e, m=m, ppt=ppt: e.tensor_copy(
                            ya_sb[:, m * 512:(m + 1) * 512], ppt[:, :])],
                        reads=[pp], writes=[R["ya"]])
            p.mark("kv")
            if n + 1 < n_sb:
                for mk in range(8):
                    for s2 in range(2):
                        j2 = (mk * 2 + s2) % 4
                        pp, ppt = rps[j2], ps[j2]
                        p.group("tensor",
                                [lambda e, jj=jj, mk=mk, s2=s2, ppt=ppt: e.matmul(
                                    ppt[:, :],
                                    lnf[:, 2 * jj + mk // 4,
                                        (mk % 4) * 128:(mk % 4 + 1) * 128],
                                    v_sb[:, jj * D + s2 * 512:jj * D + (s2 + 1) * 512],
                                    start=(jj == 0), stop=(jj == 3))
                                 for jj in range(NTT)],
                                reads=r_lnf + [R["v"]], writes=[pp])
                        sl = (mk * D + s2 * 512, mk * D + (s2 + 1) * 512)
                        if n == 0:
                            p.group("vector",
                                    [lambda e, sl=sl, ppt=ppt: e.tensor_copy(
                                        S_sb[:, sl[0]:sl[1]], ppt[:, :])],
                                    reads=[pp], writes=[R["S"]])
                        else:
                            p.group("vector",
                                    [lambda e, sl=sl, ppt=ppt: e.tensor_tensor(
                                        S_sb[:, sl[0]:sl[1]], ppt[:, :],
                                        S_sb[:, sl[0]:sl[1]], ALU.add)],
                                    reads=[pp], writes=[R["S"]])
            p.mark("wo")
            ln_c = LN("bc", X, rX)
            for i in range(NTT):
                for s2 in range(2):
                    j2 = (i * 2 + s2) % 4
                    pp, ppt = rps[j2], ps[j2]
                    p.group("tensor",
                            [lambda e, c2=c2, i=i, s2=s2, ppt=ppt: e.matmul(
                                ppt[:, :],
                                ya_sb[:, c2 * 512 + i * 128:c2 * 512 + (i + 1) * 128],
                                sb["wo"][:, c2 * D + s2 * 512:c2 * D + (s2 + 1) * 512],
                                start=(c2 == 0), stop=(c2 == 7)) for c2 in range(8)],
                            reads=[R["ya"], R["wo"]], writes=[pp])
                    col = n * NTT + i
                    sl = (i * D + s2 * 512, i * D + (s2 + 1) * 512)
                    p.group("vector",
                            [lambda e, sl=sl, col=col, ppt=ppt, X=X, H=H:
                             e.scalar_tensor_tensor(
                                X[:, sl[0]:sl[1]], ppt[:, :], dq_c[:, col:col + 1],
                                H[:, sl[0]:sl[1]], ALU.mult, ALU.add)],
                            reads=[pp, rH, R["dq_c"]], writes=[rX])
                    ln_c.feed2(i, s2)
            tr_tile(0)
            tr_tile(1)
            ln_c.flush()

            p.mark("trC")
            p.mark("ffnC")
            if n + 1 < n_sb:
                PB, rPB = B[XI[(n + 1) % 3]], rB[XI[(n + 1) % 3]]
                ln_a = LN("a", PB, rPB, dma_src=True)
            else:
                ln_a = None
            rH2 = [Res("H%d_h%d" % (n, h)) for h in range(2)]

            def store_cb(i, n=n, H=H, rH2=rH2):
                if i in (1, 3):
                    h = i // 2
                    p.dma(lambda n=n, h=h: out_d[:, n * NTT * D + h * 2 * D:
                                                 n * NTT * D + (h + 1) * 2 * D],
                          lambda H=H, h=h: H[:, h * 2 * D:(h + 1) * 2 * D],
                          reads=[rH2[h]], sem="st")
            ffn(H, rH, X, rX, ln_a, half_res=rH2, tile_cb=store_cb,
                skip_tr01=True, no_flush=True)
            if ln_a is not None:
                tr_tile(0)
                tr_tile(1)
                ln_a.flush()

        p.items["sync"].append(("wait", "st", p.cnt["st"]))
        LAST_PLAN = p

        with nc.Block() as block:
            for eng in ("sync", "tensor", "vector", "scalar", "gpsimd"):
                items = p.items[eng]
                if not items:
                    continue

                def body(e, items=items):
                    for it in items:
                        if it[0] == "wait":
                            e.wait_ge(S[it[1]], it[2])
                        else:
                            inst = it[1](e)
                            if it[2] is not None:
                                inst.then_inc(S[it[2][0]], it[2][1])
                getattr(block, eng)(body)
    return nc


def _tile128(w):
    n = w.shape[1]
    return np.ascontiguousarray(
        w.reshape(8, 128, n).transpose(1, 0, 2).reshape(128, 8 * n))


def _hilo(vals64):
    c = np.ascontiguousarray(vals64.reshape(32, 128).T)
    hi = c.astype(np.float32)
    lo = (c - hi.astype(np.float64)).astype(np.float32)
    return hi, lo


def kernel(L0_attn_x, L0_attn_Wv, L0_attn_Wk, L0_attn_bk, L0_attn_Wq,
           L0_attn_bq, L0_attn_Wo, L0_ffn_weight, L0_ffn_disha):
    bf16 = ml_dtypes.bfloat16
    x = np.asarray(L0_attn_x, dtype=np.float32)
    wq = _tile128(np.asarray(L0_attn_Wq, dtype=np.float32))
    wk = _tile128(np.asarray(L0_attn_Wk, dtype=np.float32))
    wv = _tile128(np.asarray(L0_attn_Wv, dtype=np.float32))
    wo = _tile128(np.asarray(L0_attn_Wo, dtype=np.float32))
    wt = _tile128(np.ascontiguousarray(np.asarray(L0_ffn_weight, dtype=np.float32).T))
    disha = np.asarray(L0_ffn_disha, dtype=np.float32)
    disha2 = np.ascontiguousarray(np.concatenate([disha, disha], axis=0))
    bq = np.ascontiguousarray(np.asarray(L0_attn_bq, np.float32).reshape(8, 128).T)
    bk = np.ascontiguousarray(np.asarray(L0_attn_bk, np.float32).reshape(8, 128).T)
    s64 = np.arange(T, dtype=np.float64)
    dqh, dql = _hilo(np.float64(DECAY) ** s64)
    dkh, dkl = _hilo(np.float64(DECAY) ** (-s64))
    tril = np.triu(np.ones((128, 128))).astype(bf16)
    ident = np.eye(128).astype(bf16)

    nc = build_nc()
    in_maps = []
    for c in range(8):
        xc = np.ascontiguousarray(
            x[c].reshape(32, 128, D).transpose(1, 0, 2).reshape(128, 32 * D))
        in_maps.append({"xin": xc, "wq": wq, "wk": wk, "wv": wv, "wo": wo,
                        "wt": wt, "disha2": disha2, "bq": bq, "bk": bk,
                        "dqh": dqh, "dql": dql, "dkh": dkh, "dkl": dkl,
                        "tril": tril, "ident": ident})
    res = run_bass_kernel_spmd(nc, in_maps, core_ids=list(range(8)))
    outs = []
    for r in res.results:
        o = np.asarray(r["out"]).reshape(128, 32, D).transpose(1, 0, 2)
        outs.append(o.reshape(T, D))
    return np.stack(outs).astype(np.float32)


if __name__ == "__main__":
    import reference
    ins = reference.setup_inputs()
    out = kernel(**{k: np.asarray(v) for k, v in ins.items()})
    ref = np.asarray(reference.reference(**ins))
    err = np.linalg.norm(out - ref) / np.linalg.norm(ref)
    print("rel_err", err)


# version 31
# speedup vs baseline: 1.0496x; 1.0310x over previous
import contextlib
import numpy as np
import ml_dtypes
import concourse.bass as bass
import concourse.mybir as mybir
from concourse.bass_utils import run_bass_kernel_spmd

F32 = mybir.dt.float32
BF16 = mybir.dt.bfloat16
AF = mybir.ActivationFunctionType
ALU = mybir.AluOpType

T, D = 4096, 1024
NSB, SBT, NTT = 8, 512, 4
DECAY = 0.99
EPS = 1e-5
APP_SPLIT = False
TR_COPY_SPLIT = False

SEMS = ["pe", "dv", "ac", "gp", "lx", "gx", "st", "ld0", "ld1", "ld2", "ld3"]
ENG_OF_SEM = {"pe": "tensor", "dv": "vector", "ac": "scalar", "gp": "gpsimd",
              "lx": "sync", "gx": "sync", "st": "sync", "ld0": "sync",
              "ld1": "sync", "ld2": "sync", "ld3": "sync"}
ENG_SEM = {"tensor": "pe", "vector": "dv", "scalar": "ac", "gpsimd": "gp"}
COMPUTE_SEMS = ("pe", "dv", "ac", "gp")


class Res:
    __slots__ = ("name", "w", "r")

    def __init__(self, name):
        self.name, self.w, self.r = name, None, {}


class Plan:
    def __init__(self):
        self.items = {e: [] for e in ("tensor", "vector", "scalar", "sync", "gpsimd")}
        self.cnt = {s: 0 for s in SEMS}
        self.floor = {e: {} for e in self.items}
        self.ld_rr = 0
        self.marks = {e: [] for e in self.items}
        self.ncall = {e: 0 for e in self.items}

    def mark(self, label):
        for e in self.marks:
            self.marks[e].append((len(self.items[e]), label))

    def group(self, eng, fns, reads=(), writes=(), sem=None, dma=False, record=()):
        if sem is None:
            sem = ENG_SEM[eng]
        amt = 16 if dma else 1
        waits = {}

        def need(s, v):
            if s in COMPUTE_SEMS and ENG_OF_SEM[s] == eng:
                return
            if v > waits.get(s, 0):
                waits[s] = v

        ncalls = self.ncall[eng]
        for r in reads:
            if r.w is not None:
                need(r.w[0], r.w[1])
                if r.w[2] == eng and eng != "sync" and ncalls - r.w[3] < 2:
                    raise AssertionError(
                        "same-engine RAW too close: %s dist=%d eng=%s"
                        % (r.name, ncalls - r.w[3], eng))
        for r in writes:
            if r.w is not None:
                need(r.w[0], r.w[1])
            for s, v in r.r.items():
                need(s, v)
        if dma:
            if self.cnt[sem] > 0:
                need(sem, self.cnt[sem])
        fl = self.floor[eng]
        for s in sorted(waits):
            v = waits[s]
            if fl.get(s, 0) >= v:
                continue
            self.items[eng].append(("wait", s, v))
            fl[s] = v
        self.cnt[sem] += amt
        val = self.cnt[sem]
        for i, fn in enumerate(fns):
            self.items[eng].append(
                ("call", fn, (sem, amt) if i == len(fns) - 1 else None))
        self.ncall[eng] += len(fns)
        for r in writes:
            r.w = (sem, val, eng, self.ncall[eng])
            r.r = {}
        for r in reads:
            if val > r.r.get(sem, 0):
                r.r[sem] = val
        for r in record:
            if val > r.r.get(sem, 0):
                r.r[sem] = val
        return (sem, val)

    def dma(self, out_fn, in_fn, reads=(), writes=(), sem=None, eng="sync"):
        if sem is None:
            sem = "ld%d" % (self.ld_rr % 4)
            self.ld_rr += 1
        return self.group(eng, [lambda e: e.dma_start(out=out_fn(), in_=in_fn())],
                          reads=reads, writes=writes, sem=sem, dma=True)


LAST_PLAN = None


def build_nc(n_sb=NSB):
    global LAST_PLAN
    nc = bass.Bass("TRN2", target_bir_lowering=False, detect_race_conditions=False)
    dt = {}

    def din(name, shape, dty=F32):
        dt[name] = nc.dram_tensor(name, shape, dty, kind="ExternalInput").ap()

    din("xin", [128, T * D // 128])
    for w in ("wq", "wk", "wv", "wo", "wt"):
        din(w, [128, 8 * D])
    din("disha2", [128, D])
    din("bq", [128, 8])
    din("bk", [128, 8])
    for nm in ("dqh", "dql", "dkh", "dkl"):
        din(nm, [128, 32])
    din("tril", [128, 128], BF16)
    din("ident", [128, 128], BF16)
    out_d = nc.dram_tensor("out", [128, T * D // 128], F32, kind="ExternalOutput").ap()

    p = Plan()
    with contextlib.ExitStack() as ctx:
        S = {s: ctx.enter_context(nc.semaphore(s)) for s in SEMS}
        sb = {}

        def sbuf(name, shape, dty):
            sb[name] = ctx.enter_context(nc.sbuf_tensor("s_" + name, shape, dty))
            return sb[name]

        for w in ("wq", "wk", "wv", "wo", "wf"):
            sbuf(w, [128, 8 * D], BF16)
        S_sb = sbuf("S", [128, 8 * D], BF16)
        B = [sbuf("B%d" % i, [128, NTT * D], F32) for i in range(3)]
        ln_tm = sbuf("ln_tm", [128, NTT * D], BF16)
        lnf = sbuf("lnf", [128, 8, SBT], BF16)
        q_sb = sbuf("q", [128, 8 * SBT], BF16)
        kf_sb = sbuf("kf", [128, 8 * SBT], BF16)
        v_sb = sbuf("v", [128, NTT * D], BF16)
        sc_sb = sbuf("sc", [128, NTT * SBT], BF16)
        ya_sb = sbuf("ya", [128, 8 * SBT], BF16)
        dq_c = sbuf("dq_c", [128, 32], F32)
        dk_c = sbuf("dk_c", [128, 32], F32)
        ct0 = sbuf("ct0", [128, 32], F32)
        ct1 = sbuf("ct1", [128, 32], F32)
        bq_sb = sbuf("bq_sb", [128, 8], F32)
        bk_sb = sbuf("bk_sb", [128, 8], F32)
        tril_sb = sbuf("tril_sb", [128, 128], BF16)
        id_sb = sbuf("id_sb", [128, 128], BF16)
        dsh_sb = sbuf("dsh", [128, D], F32)
        ec = sbuf("ec", [128, 4], F32)
        stat = {tag: {nm: sbuf(tag + "_" + nm, [128, w], F32)
                      for nm, w in (("stats", 48), ("mv", 8), ("negm", 16),
                                    ("vpe", 4), ("rstd", 64), ("negms", 16))}
                for tag in ("a", "bc")}
        scr = sbuf("scr", [128, 4], F32)
        scrA = sbuf("scrA", [128, 4], F32)
        wide = sbuf("wide", [128, 512], F32)
        neghalf = sbuf("neghalf", [128, 4], F32)
        scrG = sbuf("scrG", [128, 16], F32)

        ps = [ctx.enter_context(nc.psum_tensor("p%d" % i, [128, 512], F32))
              for i in range(4)]
        pt = [ctx.enter_context(nc.psum_tensor("t%d" % i, [128, 1024], BF16))
              for i in range(4)]

        R = {n: Res(n) for n in
             list(sb) + ["p%d" % i for i in range(4)] + ["t%d" % i for i in range(4)]}
        rB = [R["B0"], R["B1"], R["B2"]]
        rps = [R["p%d" % i] for i in range(4)]
        rpt = [R["t%d" % i] for i in range(4)]
        r_ln_a = [Res("lna_%d" % i) for i in range(NTT)]
        r_ln_d = [Res("lnd_%d" % i) for i in range(NTT)]
        r_lnf = [Res("lnf_%d" % i) for i in range(NTT)]
        rstat = {tag: {nm: [Res(tag + nm + str(i)) for i in range(NTT)]
                       for nm in ("stats", "mv", "negm", "vpe", "rstd",
                                  "negms")}
                 for tag in ("a", "bc")}

        p.dma(lambda: B[0][:, :], lambda: dt["xin"][:, 0:NTT * D],
              writes=[rB[0]], sem="lx")
        for name, dest in (("ident", id_sb), ("tril", tril_sb), ("bq", bq_sb),
                           ("bk", bk_sb), ("disha2", dsh_sb)):
            p.dma(lambda d=dest: d[:, :], lambda n=name: dt[n][:, :],
                  writes=[R[{"ident": "id_sb", "tril": "tril_sb", "bq": "bq_sb",
                             "bk": "bk_sb", "disha2": "dsh"}[name]]])
        p.group("vector", [lambda e: e.memset(ec[:, :], EPS)], writes=[R["ec"]])
        p.group("vector", [lambda e: e.memset(neghalf[:, :], -0.5)],
                writes=[R["neghalf"]])
        p.group("vector", [lambda e: e.memset(wide[:, :], 0.0)],
                writes=[R["wide"]])

        slot_res = [Res("wslot%d" % i) for i in range(2)]

        def slot_ap(i):
            return B[2][:, i * 2048:(i + 1) * 2048]
        slot_rb = [rB[2], rB[2]]
        w_order = ("wt", "wq", "wk", "wv", "wo")
        w_scale = {"wq": 1.0, "wk": 1.0, "wv": 0.01, "wo": 1.0}
        w_eng = {"wq": "scalar", "wk": "scalar", "wv": "scalar", "wo": "vector"}

        LOADS = [(wi, q) for wi in range(5) for q in range(4)]
        dma_ptr = [0]
        cast_ptr = [0]

        def pump(k):
            for _ in range(k):
                if dma_ptr[0] >= len(LOADS):
                    return
                wi, q = LOADS[dma_ptr[0]]
                dma_ptr[0] += 1
                sl = (wi * 4 + q) % 2
                p.dma(lambda sl=sl: slot_ap(sl),
                      lambda w=w_order[wi], q=q: dt[w][:, q * 2048:(q + 1) * 2048],
                      writes=[slot_res[sl]])

        def cast_next(m):
            for _ in range(m):
                wi, q = LOADS[cast_ptr[0]]
                cast_ptr[0] += 1
                sl = (wi * 4 + q) % 2
                w = w_order[wi]
                if w == "wt":
                    p.group("vector",
                            [lambda e, sl=sl, q=q, cc=cc: e.scalar_tensor_tensor(
                                sb["wf"][:, q * 2048 + cc * D:q * 2048 + (cc + 1) * D],
                                slot_ap(sl)[:, cc * D:(cc + 1) * D], 1.0,
                                dsh_sb[:, :], ALU.mult, ALU.add)
                             for cc in range(2)],
                            reads=[slot_res[sl], R["dsh"]], record=[rB[2]],
                            writes=[R["wf"]])
                elif w_eng[w] == "scalar":
                    p.group("scalar",
                            [lambda e, sl=sl, w=w, q=q: e.activation(
                                sb[w][:, q * 2048:(q + 1) * 2048], slot_ap(sl),
                                AF.Copy, scale=w_scale[w])],
                            reads=[slot_res[sl]], record=[rB[2]], writes=[R[w]])
                else:
                    p.group("vector",
                            [lambda e, sl=sl, w=w, q=q: e.tensor_scalar(
                                sb[w][:, q * 2048:(q + 1) * 2048], slot_ap(sl),
                                w_scale[w], None, ALU.mult)],
                            reads=[slot_res[sl]], record=[rB[2]], writes=[R[w]])
                pump(1)

        XI = [0, 2, 1]
        HI = [1, 0, 2]

        class LN:

            def __init__(self, tag, xbuf, rx, dma_src=False, rx_tiles=None):
                self.st, self.rr = stat[tag], rstat[tag]
                self.x, self.rx, self.dma = xbuf, rx, dma_src
                self.rxt = rx_tiles

            def s1a(self, i):
                st, x = self.st, self.x
                p.group("vector",
                        [lambda e: e.bn_stats(st["stats"][:, 12 * i:12 * i + 6],
                                              x[:, i * D:i * D + 512])],
                        reads=([self.rxt[i]] if self.rxt else
                               ([self.rx] if self.dma else [])),
                        writes=[self.rr["stats"][i]])

            def s1b(self, i):
                st, x = self.st, self.x
                p.group("vector",
                        [lambda e: e.bn_stats(st["stats"][:, 12 * i + 6:12 * i + 12],
                                              x[:, i * D + 512:(i + 1) * D])],
                        reads=([self.rxt[i]] if self.rxt else
                               ([self.rx] if self.dma else [])),
                        writes=[self.rr["stats"][i]])

            def s2(self, i):
                st = self.st
                p.group("vector",
                        [lambda e: e.bn_aggr(st["mv"][:, 2 * i:2 * i + 2],
                                             st["stats"][:, 12 * i:12 * i + 12])],
                        reads=[self.rr["stats"][i]], writes=[self.rr["mv"][i]])

            def s3(self, i):
                st = self.st
                p.group("vector",
                        [lambda e: e.tensor_scalar(st["negm"][:, 4 * i:4 * i + 1],
                                                   st["mv"][:, 2 * i:2 * i + 1],
                                                   -1.0, None, ALU.mult),
                         lambda e: e.tensor_scalar(st["vpe"][:, i:i + 1],
                                                   st["mv"][:, 2 * i + 1:2 * i + 2],
                                                   EPS, None, ALU.add)],
                        reads=[self.rr["mv"][i]],
                        writes=[self.rr["negm"][i], self.rr["vpe"][i]])

            def pw(self, i):
                st = self.st
                p.group("gpsimd",
                        [lambda e: e.tensor_tensor(st["rstd"][:, 16 * i:16 * i + 1],
                                                   st["vpe"][:, i:i + 1],
                                                   neghalf[:, 0:1], ALU.pow)],
                        reads=[self.rr["vpe"][i], R["neghalf"]],
                        writes=[self.rr["rstd"][i]])

            def ngs(self, i):
                st = self.st
                p.group("scalar",
                        [lambda e: e.activation(st["negms"][:, 4 * i:4 * i + 1],
                                                st["negm"][:, 4 * i:4 * i + 1], AF.Identity,
                                                scale=st["rstd"][:, 16 * i:16 * i + 1])],
                        reads=[self.rr["negm"][i], self.rr["rstd"][i]],
                        writes=[self.rr["negms"][i]])

            def app_a(self, i):
                st, x = self.st, self.x
                p.group("scalar",
                        [lambda e: e.activation(
                            ln_tm[:, i * D:i * D + 512],
                            x[:, i * D:i * D + 512],
                            AF.Identity, bias=st["negms"][:, 4 * i:4 * i + 1],
                            scale=st["rstd"][:, 16 * i:16 * i + 1])],
                        reads=[self.rr["negms"][i], self.rr["rstd"][i]],
                        record=[self.rx], writes=[r_ln_a[i]])

            def app_d(self, i):
                st, x = self.st, self.x
                p.group("gpsimd",
                        [lambda e: e.tensor_scalar(
                            ln_tm[:, i * D + 512:(i + 1) * D],
                            x[:, i * D + 512:(i + 1) * D],
                            st["negm"][:, 4 * i:4 * i + 1],
                            st["rstd"][:, 16 * i:16 * i + 1],
                            ALU.add, ALU.mult)],
                        reads=[self.rr["negm"][i], self.rr["rstd"][i]],
                        record=[self.rx], writes=[r_ln_d[i]])

            def gpad(self):
                p.group("gpsimd",
                        [lambda e: e.tensor_tensor(scrG[:, 0:1],
                                                   neghalf[:, 0:1],
                                                   neghalf[:, 1:2], ALU.mult)],
                        reads=[R["neghalf"]])

            def pad(self):
                p.group("vector", [lambda e: e.memset(scr[:, 0:1], 0.0)])

            def acpad(self):
                p.group("scalar",
                        [lambda e: e.activation(scrA[:, 0:1], ec[:, 0:1],
                                                AF.Copy)],
                        reads=[R["ec"]])

            def pad_w(self):
                p.group("vector",
                        [lambda e: e.tensor_scalar(wide[:, 256:512],
                                                   wide[:, 0:256], 1.0, None,
                                                   ALU.mult)],
                        reads=[R["wide"]])

            def acpad_w(self):
                p.group("scalar",
                        [lambda e: e.activation(wide[:, 256:512],
                                                wide[:, 0:256], AF.Copy)],
                        reads=[R["wide"]])

            def feed2(self, i, s2):
                if (i, s2) == (0, 1):
                    self.pad(); self.pad(); self.s1a(0); self.s1b(0)
                elif (i, s2) == (1, 0):
                    self.pad(); self.s2(0)
                elif (i, s2) == (1, 1):
                    self.pad(); self.s3(0); self.s1a(1); self.s1b(1)
                    self.pw(0); self.ngs(0)
                elif (i, s2) == (2, 0):
                    self.pad(); self.s2(1)
                elif (i, s2) == (2, 1):
                    self.pad(); self.s3(1); self.s1a(2); self.s1b(2)
                    self.pw(1); self.gpad(); self.ngs(1); self.acpad()
                    self.app_a(0); self.app_d(0)
                elif (i, s2) == (3, 0):
                    self.pad(); self.s2(2)
                elif (i, s2) == (3, 1):
                    self.pad(); self.s3(2); self.s1a(3); self.s1b(3)
                    self.pw(2); self.gpad(); self.ngs(2)
                    self.app_a(1); self.app_d(1)

            def flush(self):
                self.pad_w(); self.pad_w(); self.s2(3)
                self.gpad(); self.acpad(); self.app_a(2); self.app_d(2)
                self.pad_w(); self.pad_w(); self.s3(3); self.pw(3)
                self.gpad(); self.gpad()
                self.ngs(3); self.acpad_w(); self.acpad_w()
                self.app_a(3); self.app_d(3)

            def standalone(self):
                self.s1a(0); self.s1b(0); self.s1a(1); self.s1b(1)
                self.s2(0); self.s1a(2); self.s1b(2); self.s2(1)
                self.s3(0); self.pw(0); self.s1a(3); self.s1b(3)
                self.s2(2); self.s3(1); self.pw(1); self.ngs(0)
                self.s2(3); self.s3(2); self.pw(2); self.ngs(1)
                self.acpad(); self.app_a(0); self.app_d(0)
                self.s3(3); self.pw(3); self.gpad(); self.ngs(2)
                self.app_a(1); self.app_d(1); self.ngs(3)
                self.app_a(2); self.gpad(); self.app_d(2)
                self.acpad(); self.app_a(3); self.app_d(3)

        def tr_tile(i):
            if True:
                tp, ptb = rpt[i % 4], pt[i % 4]
                p.group("tensor",
                        [lambda e, c=c, i=i, ptb=ptb: e.transpose(
                            ptb[:, c * 128:(c + 1) * 128],
                            ln_tm[:, i * D + c * 128:i * D + (c + 1) * 128],
                            id_sb[:, :]) for c in range(8)],
                        reads=[r_ln_a[i], r_ln_d[i], R["id_sb"]], writes=[tp])
                p.group("vector",
                        [lambda e, i=i, ptb=ptb: e.tensor_copy(
                            lnf[:, :, i * 128:(i + 1) * 128], ptb[:, :])],
                        reads=[tp], writes=[r_lnf[i]])

        def tr_phase():
            for i in range(NTT):
                tr_tile(i)

        def ffn(dst, rdst, resid, rresid, ln, half_res=None, tile_cb=None,
                skip_tr01=False, no_flush=False):
            if not skip_tr01:
                tr_tile(0)
                tr_tile(1)
            for i in range(NTT):
                for s2 in range(2):
                    j = (i * 2 + s2) % 4
                    pp, ppt = rps[j], ps[j]
                    p.group("tensor",
                            [lambda e, c=c, i=i, s2=s2, ppt=ppt: e.matmul(
                                ppt[:, :],
                                lnf[:, c, i * 128:(i + 1) * 128],
                                sb["wf"][:, c * D + s2 * 512:c * D + (s2 + 1) * 512],
                                start=(c == 0), stop=(c == 7)) for c in range(8)],
                            reads=[R["wf"], r_lnf[i]], writes=[pp])
                    sl = (i * D + s2 * 512, i * D + (s2 + 1) * 512)
                    p.group("vector",
                            [lambda e, sl=sl, ppt=ppt: e.scalar_tensor_tensor(
                                dst[:, sl[0]:sl[1]], ppt[:, :], 0.5,
                                resid[:, sl[0]:sl[1]], ALU.mult, ALU.add)],
                            reads=[pp, rresid],
                            writes=([rdst] + ([half_res[i // 2]]
                                              if half_res else [])))
                    if ln is not None:
                        ln.feed2(i, s2)
                if i + 2 < NTT:
                    tr_tile(i + 2)
                if tile_cb is not None:
                    tile_cb(i)
            if ln is not None and not no_flush:
                ln.flush()

        LN("a", B[0], rB[0], dma_src=True).standalone()
        for hi, lo, dst in (("dqh", "dql", "dq_c"), ("dkh", "dkl", "dk_c")):
            p.dma(lambda: ct0[:, :], lambda n=hi: dt[n][:, :], writes=[R["ct0"]])
            p.dma(lambda: ct1[:, :], lambda n=lo: dt[n][:, :], writes=[R["ct1"]])
            p.group("vector",
                    [lambda e, d=dst: e.tensor_tensor(sb[d][:, :], ct0[:, :],
                                                      ct1[:, :], ALU.add)],
                    reads=[R["ct0"], R["ct1"]], writes=[R[dst]])
        pump(2)
        cast_next(4)

        for n in range(n_sb):
            p.mark("S%d" % n)
            X, H = B[XI[n % 3]], B[HI[n % 3]]
            rX, rH = rB[XI[n % 3]], rB[HI[n % 3]]
            if 1 <= n and n + 1 < n_sb:
                PB, rPB = B[XI[(n + 1) % 3]], rB[XI[(n + 1) % 3]]
                p.dma(lambda pb=PB: pb[:, :],
                      lambda m=n + 1: dt["xin"][:, m * NTT * D:(m + 1) * NTT * D],
                      writes=[rPB], sem="lx")

            p.mark("trA")
            if n == 0:
                cast_next(4)
            p.mark("ffnA")
            ln_b = LN("bc", H, rH)
            ffn(H, rH, X, rX, ln_b, skip_tr01=(n > 0), no_flush=True)
            if n == 0:
                cast_next(4)

            p.mark("trB")
            if n == 0:
                cast_next(4)
            p.mark("v")
            tr_tile(0)
            tr_tile(1)
            ln_b.flush()
            for i in range(NTT):
                for s2 in range(2):
                    j = (i * 2 + s2) % 4
                    pp, ppt = rps[j], ps[j]
                    p.group("tensor",
                            [lambda e, c=c, i=i, s2=s2, ppt=ppt: e.matmul(
                                ppt[:, :],
                                lnf[:, c, i * 128:(i + 1) * 128],
                                sb["wv"][:, c * D + s2 * 512:c * D + (s2 + 1) * 512],
                                start=(c == 0), stop=(c == 7)) for c in range(8)],
                            reads=[R["wv"], r_lnf[i]], writes=[pp])
                    p.group("scalar",
                            [lambda e, i=i, s2=s2, ppt=ppt: e.activation(
                                v_sb[:, i * D + s2 * 512:i * D + (s2 + 1) * 512],
                                ppt[:, :], AF.Copy)],
                            reads=[pp], writes=[R["v"]])
                if i + 2 < NTT:
                    tr_tile(i + 2)
            p.mark("qk")
            for dst, rdst, wname, bias in ((q_sb, R["q"], "wq", bq_sb),
                                           (kf_sb, R["kf"], "wk", bk_sb)):
                for c2 in range(8):
                    pp, ppt = rps[c2 % 4], ps[c2 % 4]
                    p.group("tensor",
                            [lambda e, c=c, c2=c2, wname=wname, ppt=ppt: e.matmul(
                                ppt[:, :],
                                sb[wname][:, c * D + c2 * 128:c * D + (c2 + 1) * 128],
                                lnf[:, c, :],
                                start=(c == 0), stop=(c == 7)) for c in range(8)],
                            reads=[R[wname]] + r_lnf, writes=[pp])
                    p.group("scalar",
                            [lambda e, c2=c2, dst=dst, bias=bias, ppt=ppt: e.activation(
                                dst[:, c2 * 512:(c2 + 1) * 512], ppt[:, :], AF.Sigmoid,
                                bias=bias[:, c2:c2 + 1])],
                            reads=[pp, R["bq_sb"], R["bk_sb"]], writes=[rdst])
            if n == 0:
                cast_next(4)
                if n_sb > 1:
                    p.dma(lambda: B[2][:, :],
                          lambda: dt["xin"][:, NTT * D:2 * NTT * D],
                          reads=[slot_res[0], slot_res[1]], writes=[rB[2]],
                          sem="lx")
            p.mark("kt")
            for g in range(NTT):
                tp, ptb = rpt[g % 4], pt[g % 4]
                p.group("tensor",
                        [lambda e, c2=c2, g=g, ptb=ptb: e.transpose(
                            ptb[:, c2 * 128:(c2 + 1) * 128],
                            kf_sb[:, c2 * 512 + g * 128:c2 * 512 + (g + 1) * 128],
                            id_sb[:, :]) for c2 in range(8)],
                        reads=[R["kf"], R["id_sb"]], writes=[tp])
                col = n * NTT + g
                p.group("scalar",
                        [lambda e, g=g, h2=h2, ptb=ptb, col=col: e.activation(
                            lnf[:, 2 * g + h2, :], ptb[:, h2 * 512:(h2 + 1) * 512],
                            AF.Identity, scale=dk_c[:, col:col + 1])
                         for h2 in range(2)],
                        reads=[tp, R["dk_c"]], writes=[r_lnf[g]])
            p.mark("sc")
            for j in range(NTT):
                pp, ppt = rps[j % 4], ps[j % 4]
                p.group("tensor",
                        [lambda e, c2=c2, j=j, ppt=ppt: e.matmul(
                            ppt[:, j * 128:512],
                            kf_sb[:, c2 * 512 + j * 128:c2 * 512 + (j + 1) * 128],
                            q_sb[:, c2 * 512 + j * 128:(c2 + 1) * 512],
                            start=(c2 == 0), stop=(c2 == 7)) for c2 in range(8)],
                        reads=[R["kf"], R["q"]], writes=[pp])
                col = n * NTT + j
                fns = []
                fns.append(lambda e, j=j, col=col, ppt=ppt: e.scalar_tensor_tensor(
                    sc_sb[:, j * 512 + j * 128:j * 512 + (j + 1) * 128],
                    ppt[:, j * 128:(j + 1) * 128], dk_c[:, col:col + 1],
                    tril_sb[:, :], ALU.mult, ALU.mult))
                if j < 3:
                    fns.append(lambda e, j=j, col=col, ppt=ppt: e.tensor_scalar(
                        sc_sb[:, j * 512 + (j + 1) * 128:(j + 1) * 512],
                        ppt[:, (j + 1) * 128:512], dk_c[:, col:col + 1], None,
                        ALU.mult))
                p.group("vector", fns, reads=[pp, R["dk_c"], R["tril_sb"]],
                        writes=[R["sc"]])
            p.mark("y")
            for m in range(8):
                pp, ppt = rps[m % 4], ps[m % 4]
                fns = [lambda e, jj=jj, m=m, ppt=ppt, n=n: e.matmul(
                    ppt[:, jj * 128:512],
                    v_sb[:, jj * D + m * 128:jj * D + (m + 1) * 128],
                    sc_sb[:, jj * 512 + jj * 128:(jj + 1) * 512],
                    start=(jj == 0), stop=(jj == 3 and n == 0)) for jj in range(NTT)]
                if n > 0:
                    fns += [lambda e, c2=c2, m=m, ppt=ppt: e.matmul(
                        ppt[:, :], S_sb[:, c2 * D + m * 128:c2 * D + (m + 1) * 128],
                        q_sb[:, c2 * 512:(c2 + 1) * 512],
                        start=False, stop=(c2 == 7)) for c2 in range(8)]
                p.group("tensor", fns, reads=[R["v"], R["sc"], R["S"], R["q"]],
                        writes=[pp])
                p.group("vector",
                        [lambda e, m=m, ppt=ppt: e.tensor_copy(
                            ya_sb[:, m * 512:(m + 1) * 512], ppt[:, :])],
                        reads=[pp], writes=[R["ya"]])
            p.mark("kv")
            if n + 1 < n_sb:
                for mk in range(8):
                    for s2 in range(2):
                        j2 = (mk * 2 + s2) % 4
                        pp, ppt = rps[j2], ps[j2]
                        p.group("tensor",
                                [lambda e, jj=jj, mk=mk, s2=s2, ppt=ppt: e.matmul(
                                    ppt[:, :],
                                    lnf[:, 2 * jj + mk // 4,
                                        (mk % 4) * 128:(mk % 4 + 1) * 128],
                                    v_sb[:, jj * D + s2 * 512:jj * D + (s2 + 1) * 512],
                                    start=(jj == 0), stop=(jj == 3))
                                 for jj in range(NTT)],
                                reads=r_lnf + [R["v"]], writes=[pp])
                        sl = (mk * D + s2 * 512, mk * D + (s2 + 1) * 512)
                        if n == 0:
                            p.group("vector",
                                    [lambda e, sl=sl, ppt=ppt: e.tensor_copy(
                                        S_sb[:, sl[0]:sl[1]], ppt[:, :])],
                                    reads=[pp], writes=[R["S"]])
                        else:
                            p.group("vector",
                                    [lambda e, sl=sl, ppt=ppt: e.tensor_tensor(
                                        S_sb[:, sl[0]:sl[1]], ppt[:, :],
                                        S_sb[:, sl[0]:sl[1]], ALU.add)],
                                    reads=[pp], writes=[R["S"]])
            p.mark("wo")
            ln_c = LN("bc", X, rX)
            for i in range(NTT):
                for s2 in range(2):
                    j2 = (i * 2 + s2) % 4
                    pp, ppt = rps[j2], ps[j2]
                    p.group("tensor",
                            [lambda e, c2=c2, i=i, s2=s2, ppt=ppt: e.matmul(
                                ppt[:, :],
                                ya_sb[:, c2 * 512 + i * 128:c2 * 512 + (i + 1) * 128],
                                sb["wo"][:, c2 * D + s2 * 512:c2 * D + (s2 + 1) * 512],
                                start=(c2 == 0), stop=(c2 == 7)) for c2 in range(8)],
                            reads=[R["ya"], R["wo"]], writes=[pp])
                    col = n * NTT + i
                    sl = (i * D + s2 * 512, i * D + (s2 + 1) * 512)
                    p.group("vector",
                            [lambda e, sl=sl, col=col, ppt=ppt, X=X, H=H:
                             e.scalar_tensor_tensor(
                                X[:, sl[0]:sl[1]], ppt[:, :], dq_c[:, col:col + 1],
                                H[:, sl[0]:sl[1]], ALU.mult, ALU.add)],
                            reads=[pp, rH, R["dq_c"]], writes=[rX])
                    ln_c.feed2(i, s2)
            tr_tile(0)
            tr_tile(1)
            ln_c.flush()

            p.mark("trC")
            p.mark("ffnC")
            if n + 1 < n_sb:
                PB, rPB = B[XI[(n + 1) % 3]], rB[XI[(n + 1) % 3]]
                ln_a = LN("a", PB, rPB, dma_src=True)
            else:
                ln_a = None
            rH2 = [Res("H%d_h%d" % (n, h)) for h in range(2)]

            def store_cb(i, n=n, H=H, rH2=rH2):
                if i in (1, 3):
                    h = i // 2
                    p.dma(lambda n=n, h=h: out_d[:, n * NTT * D + h * 2 * D:
                                                 n * NTT * D + (h + 1) * 2 * D],
                          lambda H=H, h=h: H[:, h * 2 * D:(h + 1) * 2 * D],
                          reads=[rH2[h]], sem="st")
            ffn(H, rH, X, rX, ln_a, half_res=rH2, tile_cb=store_cb,
                skip_tr01=True, no_flush=True)
            if ln_a is not None:
                tr_tile(0)
                tr_tile(1)
                ln_a.flush()

        p.items["sync"].append(("wait", "st", p.cnt["st"]))
        LAST_PLAN = p

        with nc.Block() as block:
            for eng in ("sync", "tensor", "vector", "scalar", "gpsimd"):
                items = p.items[eng]
                if not items:
                    continue

                def body(e, items=items):
                    for it in items:
                        if it[0] == "wait":
                            e.wait_ge(S[it[1]], it[2])
                        else:
                            inst = it[1](e)
                            if it[2] is not None:
                                inst.then_inc(S[it[2][0]], it[2][1])
                getattr(block, eng)(body)
    return nc


def _tile128(w):
    n = w.shape[1]
    return np.ascontiguousarray(
        w.reshape(8, 128, n).transpose(1, 0, 2).reshape(128, 8 * n))


def _hilo(vals64):
    c = np.ascontiguousarray(vals64.reshape(32, 128).T)
    hi = c.astype(np.float32)
    lo = (c - hi.astype(np.float64)).astype(np.float32)
    return hi, lo


def kernel(L0_attn_x, L0_attn_Wv, L0_attn_Wk, L0_attn_bk, L0_attn_Wq,
           L0_attn_bq, L0_attn_Wo, L0_ffn_weight, L0_ffn_disha):
    bf16 = ml_dtypes.bfloat16
    x = np.asarray(L0_attn_x, dtype=np.float32)
    wq = _tile128(np.asarray(L0_attn_Wq, dtype=np.float32))
    wk = _tile128(np.asarray(L0_attn_Wk, dtype=np.float32))
    wv = _tile128(np.asarray(L0_attn_Wv, dtype=np.float32))
    wo = _tile128(np.asarray(L0_attn_Wo, dtype=np.float32))
    wt = _tile128(np.ascontiguousarray(np.asarray(L0_ffn_weight, dtype=np.float32).T))
    disha = np.asarray(L0_ffn_disha, dtype=np.float32)
    disha2 = np.ascontiguousarray(np.concatenate([disha, disha], axis=0))
    bq = np.ascontiguousarray(np.asarray(L0_attn_bq, np.float32).reshape(8, 128).T)
    bk = np.ascontiguousarray(np.asarray(L0_attn_bk, np.float32).reshape(8, 128).T)
    s64 = np.arange(T, dtype=np.float64)
    dqh, dql = _hilo(np.float64(DECAY) ** s64)
    dkh, dkl = _hilo(np.float64(DECAY) ** (-s64))
    tril = np.triu(np.ones((128, 128))).astype(bf16)
    ident = np.eye(128).astype(bf16)

    nc = build_nc()
    in_maps = []
    for c in range(8):
        xc = np.ascontiguousarray(
            x[c].reshape(32, 128, D).transpose(1, 0, 2).reshape(128, 32 * D))
        in_maps.append({"xin": xc, "wq": wq, "wk": wk, "wv": wv, "wo": wo,
                        "wt": wt, "disha2": disha2, "bq": bq, "bk": bk,
                        "dqh": dqh, "dql": dql, "dkh": dkh, "dkl": dkl,
                        "tril": tril, "ident": ident})
    res = run_bass_kernel_spmd(nc, in_maps, core_ids=list(range(8)))
    outs = []
    for r in res.results:
        o = np.asarray(r["out"]).reshape(128, 32, D).transpose(1, 0, 2)
        outs.append(o.reshape(T, D))
    return np.stack(outs).astype(np.float32)


if __name__ == "__main__":
    import reference
    ins = reference.setup_inputs()
    out = kernel(**{k: np.asarray(v) for k, v in ins.items()})
    ref = np.asarray(reference.reference(**ins))
    err = np.linalg.norm(out - ref) / np.linalg.norm(ref)
    print("rel_err", err)
